# version 01 (final)
# baseline (speedup 1.0000x reference)
import numpy as np
import ml_dtypes
import concourse.bass as bass
import concourse.mybir as mybir
from concourse.bass_utils import run_bass_kernel_spmd

F32 = mybir.dt.float32
F8 = mybir.dt.float8e4
BF16 = mybir.dt.bfloat16
I32 = mybir.dt.int32
AF = mybir.ActivationFunctionType
ALU = mybir.AluOpType
AX = mybir.AxisListType

D = 2048
T = 4096
NB = 4
TOWN = 2048
NDT = 16
BIG = 1.0e30
SCL0 = float(128.0 ** -0.5)
BASE_LR_INV = float(0.001 + np.log(-np.expm1(-0.001)))
ENGS = ("sync", "gpsimd", "tensor", "scalar", "vector")


def _col128(v):
    return np.ascontiguousarray(v.reshape(-1, 128).T.astype(np.float32))


def host_prep(inputs):
    f32 = np.float32
    W = {}
    W["wqkv0"] = np.ascontiguousarray(inputs["L0_attn_w_qkv"].astype(f32))
    W["wproj0"] = np.ascontiguousarray(inputs["L0_attn_w_proj"].astype(f32))
    W["gate0w"] = np.ascontiguousarray(inputs["L0_ffn_gate_w"].astype(f32))
    W["up0w"] = np.ascontiguousarray(inputs["L0_ffn_up_w"].astype(f32))
    W["down0w"] = np.ascontiguousarray(inputs["L0_ffn_down_w"].astype(f32))
    W["gs0col"] = _col128(inputs["L0_ffn_gate_scale"])
    W["us0col"] = _col128(inputs["L0_ffn_up_scale"])
    W["ds0col"] = _col128(inputs["L0_ffn_down_scale"])
    W["wqkv1T"] = np.ascontiguousarray(inputs["L1_attn_Wqkv"].astype(f32).T)
    W["woT"] = np.ascontiguousarray(inputs["L1_attn_Wo"].astype(f32).T)
    W["qnormcol"] = _col128(inputs["L1_attn_q_norm_w"])
    W["knormcol"] = _col128(inputs["L1_attn_k_norm_w"])
    qs = inputs["L1_attn_qk_scale"].astype(f32)
    qo = inputs["L1_attn_qk_offset"].astype(f32)
    W["qscol"] = _col128(qs[:, 0]); W["kscol"] = _col128(qs[:, 1])
    W["qocol"] = _col128(qo[:, 0]); W["kocol"] = _col128(qo[:, 1])
    W["ttnormcol"] = _col128(inputs["L1_attn_ttt_norm_w"])
    W["w0T"] = np.ascontiguousarray(inputs["L1_attn_w0"].astype(f32).swapaxes(1, 2))
    W["w2T"] = np.ascontiguousarray(inputs["L1_attn_w2"].astype(f32).swapaxes(1, 2))
    W["w1n"] = np.ascontiguousarray(inputs["L1_attn_w1"].astype(f32))
    W["w1T"] = np.ascontiguousarray(inputs["L1_attn_w1"].astype(f32).swapaxes(1, 2))
    W["lrwT"] = np.ascontiguousarray(inputs["L1_attn_lr_w"].astype(f32).T)
    W["lrb"] = np.ascontiguousarray(
        np.tile(inputs["L1_attn_lr_b"].astype(f32).reshape(1, 12), (128, 1)))
    W["gatewT"] = np.ascontiguousarray(inputs["L1_ffn_gate_w"].astype(f32).T)
    W["gbias"] = np.ascontiguousarray(
        np.tile(inputs["L1_ffn_gate_bias"].astype(f32).reshape(1, 32), (128, 1)))
    W["fc1T"] = np.ascontiguousarray(inputs["L1_ffn_fc1_w"].astype(f32).T)
    W["moew1T"] = np.ascontiguousarray(inputs["L1_ffn_w1"].astype(f32).swapaxes(1, 2))
    W["moew2T"] = np.ascontiguousarray(inputs["L1_ffn_w2"].astype(f32).swapaxes(1, 2))
    W["fc2T"] = np.ascontiguousarray(inputs["L1_ffn_fc2_w"].astype(f32).T)
    W["shupT"] = np.ascontiguousarray(inputs["L1_ffn_shared_up_w"].astype(f32).T)
    W["shdnT"] = np.ascontiguousarray(inputs["L1_ffn_shared_down_w"].astype(f32).T)
    hd = 128
    inv_freq = (1.0 / (500000.0 ** (np.arange(0, hd, 2, dtype=f32) / hd))).astype(f32)
    ang = (np.arange(T, dtype=f32)[:, None] * inv_freq[None, :]).astype(f32)
    W["cosfm"] = np.ascontiguousarray(np.cos(ang).astype(f32).T)
    W["sinfm"] = np.ascontiguousarray(np.sin(ang).astype(f32).T)
    W["identb"] = np.eye(128, dtype=f32).astype(ml_dtypes.bfloat16)
    W["identf"] = np.eye(128, dtype=f32)
    W["onescol"] = np.ones((128, 1), f32)
    W["onescol_b"] = np.ones((128, 1), f32).astype(ml_dtypes.bfloat16)
    W["onesrow"] = np.ones((1, 128), f32)
    ii = np.arange(128)[:, None]
    jj = np.arange(128)[None, :]
    W["dmaskb"] = np.where(jj <= ii, 0.0, -BIG).astype(f32).astype(ml_dtypes.bfloat16)
    return W


def per_core_inputs(inputs, W, bidx, role):
    f32 = np.float32
    m = dict(W)
    m["x0T"] = np.ascontiguousarray(inputs["L0_attn_x"][bidx].astype(f32).T)
    m["cfg"] = np.array([[role * TOWN, role * 16] + [0] * 6], dtype=np.int32)
    aco = np.zeros((4, 4, 12), f32)
    bco = np.zeros((4, 4, 12), f32)
    BB = float(np.asarray(-BIG, dtype=np.float32).astype(ml_dtypes.bfloat16)
               .astype(np.float32))
    for qb in range(4):
        qb_g = role * 4 + qb
        for sub in range(4):
            for kb in range(12):
                ko = qb_g * 512 - 1024 + kb * 128
                dd = (qb_g * 512 + sub * 128) - ko
                if ko < 0 or dd <= -128 or dd >= 1152:
                    a_, b_ = 0.0, BB
                elif dd == 0:
                    a_, b_ = 1.0, 0.0
                elif dd == 1024:
                    a_, b_ = -1.0, BB
                else:
                    assert 128 <= dd <= 896, (qb, sub, kb, dd)
                    a_, b_ = 0.0, 0.0
                aco[qb, sub, kb] = a_
                bco[qb, sub, kb] = b_
    m["wacoef"] = np.ascontiguousarray(aco.reshape(1, 192))
    m["wbcoef"] = np.ascontiguousarray(bco.reshape(1, 192))
    return m


class B:

    def __init__(self, nc, stk):
        self.nc = nc
        self.stk = stk
        self.ops = {e: [] for e in ENGS}
        self.sem = {}
        self.cnt = {}

    def new_sem(self, name):
        self.sem[name] = self.stk.enter_context(self.nc.semaphore(name))
        self.cnt[name] = 0

    def add(self, eng, fn):
        self.ops[eng].append(fn)

    def wait(self, eng, mark):
        if mark is None:
            return
        name, val = mark
        if val is None or val <= 0:
            return
        s = self.sem[name]
        self.add(eng, lambda E, s=s, v=val: E.wait_ge(s, v))

    def dma(self, eng, out, in_, semname="dma"):
        s = self.sem[semname]
        self.cnt[semname] += 16
        v = self.cnt[semname]

        def fn(E, o=out, i=in_, s=s):
            E.dma_start(out=o, in_=i).then_inc(s, 16)
        self.add(eng, fn)
        return (semname, v)

    def op(self, eng, emit, semname=None, k=1):
        if semname is not None:
            s = self.sem[semname]
            self.cnt[semname] += k
            v = self.cnt[semname]

            def fn(E, emit=emit, s=s, k=k):
                emit(E).then_inc(s, k)
            self.add(eng, fn)
            return (semname, v)

        def fn(E, emit=emit):
            emit(E)
        self.add(eng, fn)
        return None


def build(upto=8, dumps=()):
    nc = bass.Bass("TRN2", target_bir_lowering=False, debug=False)
    ap = {}

    def xin(name, shape, dt=F32):
        ap[name] = nc.dram_tensor(name, list(shape), dt, kind="ExternalInput").ap()

    xin("x0T", (D, T))
    xin("wqkv0", (D, 3 * D)); xin("wproj0", (D, D))
    xin("gate0w", (D, 6144)); xin("up0w", (D, 6144)); xin("down0w", (6144, D))
    xin("gs0col", (128, 48)); xin("us0col", (128, 48)); xin("ds0col", (128, 16))
    xin("wqkv1T", (D, 3 * D)); xin("woT", (D, D))
    xin("qnormcol", (128, 16)); xin("knormcol", (128, 16))
    xin("qscol", (128, 16)); xin("kscol", (128, 16))
    xin("qocol", (128, 16)); xin("kocol", (128, 16))
    xin("ttnormcol", (128, 4))
    xin("w0T", (4, 512, 512)); xin("w2T", (4, 512, 512))
    xin("w1n", (4, 512, 512)); xin("w1T", (4, 512, 512))
    xin("lrwT", (D, 12)); xin("lrb", (128, 12))
    xin("gatewT", (D, 32)); xin("gbias", (128, 32))
    xin("fc1T", (D, 1024))
    xin("moew1T", (32, 1024, 1024)); xin("moew2T", (32, 1024, 1024))
    xin("fc2T", (1024, D))
    xin("shupT", (D, 4096)); xin("shdnT", (4096, D))
    xin("cosfm", (64, T)); xin("sinfm", (64, T))
    xin("identb", (128, 128), BF16)
    xin("identf", (128, 128))
    xin("onescol", (128, 1)); xin("onescol_b", (128, 1), BF16)
    xin("onesrow", (1, 128))
    xin("dmaskb", (128, 128), BF16)
    xin("cfg", (1, 8), I32)
    xin("wacoef", (1, 192)); xin("wbcoef", (1, 192))

    sc = {}

    def scratch(name, shape, dt):
        sc[name] = nc.dram_tensor("s_" + name, list(shape), dt).ap()

    scratch("xn0", (D, T), BF16)
    scratch("g0q", (D, 6144), BF16)
    scratch("u0q", (D, 6144), BF16)
    scratch("d0q", (6144, D), BF16)
    scratch("q0", (D, T), BF16)
    scratch("k0", (D, T), BF16)
    scratch("v0t", (T, D), BF16)
    scratch("fuse", (6144, T), BF16)
    scratch("attn0", (D, T), BF16)
    scratch("x1", (D, T), F32)
    scratch("xn1b", (D, T), BF16)
    scratch("xn1f", (D, T), F32)
    scratch("q1", (D, T), BF16)
    scratch("k1", (D, T), BF16)
    scratch("v1", (D, T), BF16)
    scratch("aq", (D, T), BF16)
    scratch("akp", (D, 1024 + T), BF16)
    scratch("v1tp", (1024 + T, D), BF16)
    scratch("fq", (D, T), BF16)
    scratch("fk", (D, T), BF16)
    scratch("fv", (D, T), BF16)
    scratch("lr", (T, 16), F32)
    scratch("attn1", (D, TOWN), BF16)
    scratch("akc", (D, 3072), BF16)
    scratch("aqo", (D, TOWN), BF16)
    scratch("vc", (3072, D), BF16)
    scratch("xn1bo", (D, TOWN), BF16)
    scratch("xn1fo", (D, TOWN), F32)
    scratch("x1o", (D, TOWN), F32)
    scratch("oio", (D, TOWN), BF16)
    scratch("ylat", (1024, TOWN), F32)
    scratch("hsh", (4096, TOWN), BF16)
    scratch("dbg", (128, 4096), F32)
    scratch("oi", (D, T), BF16)

    out_final = nc.dram_tensor("out", [D, TOWN], F32, kind="ExternalOutput").ap()
    dump_aps = []
    for dname in dumps:
        dsrc = sc[dname]
        dd = nc.dram_tensor("dump_" + dname, list(dsrc.shape), dsrc.dtype,
                            kind="ExternalOutput").ap()
        dump_aps.append((dd, dsrc))

    from contextlib import ExitStack
    with ExitStack() as stk:
        e = stk.enter_context
        b = B(nc, stk)

        act = e(nc.sbuf_tensor("act", [128, NDT * T], BF16))
        wb = [e(nc.sbuf_tensor(f"wb{i}", [128, 2048], BF16)) for i in range(2)]
        wb2 = [e(nc.sbuf_tensor(f"wb2{i}", [128, 8192], BF16)) for i in range(2)]
        ob = [e(nc.sbuf_tensor(f"ob{i}", [128, 512], BF16)) for i in range(4)]
        fbv = [e(nc.sbuf_tensor(f"fbv{i}", [128, 512], F32)) for i in range(2)]
        u1 = e(nc.sbuf_tensor("u1", [128, 12288], BF16))
        mrow = e(nc.sbuf_tensor("mrow", [128, 512], F32))
        colv = e(nc.sbuf_tensor("colv", [128, 160], F32))
        cident = e(nc.sbuf_tensor("cident", [128, 128], BF16))
        cmaskb = e(nc.sbuf_tensor("cmaskb", [128, 128], BF16))
        conesc = e(nc.sbuf_tensor("conesc", [128, 1], F32))
        conesb = e(nc.sbuf_tensor("conesb", [128, 1], BF16))
        conesr = e(nc.sbuf_tensor("conesr", [1, 128], F32))
        ccfg = e(nc.sbuf_tensor("ccfg", [1, 8], I32))
        ps = [e(nc.psum_tensor(f"ps{i}", [128, 512], F32)) for i in range(7)]
        psb = e(nc.psum_tensor("psb", [128, 1024], BF16))

        xa = u1[:, 0:8192].bitcast(F32)
        srow = u1[:, 0:8192].bitcast(F32)
        prow = u1[:, 8192:12288]
        prown = u1[:, 0:4096]
        ptbv = [wb[0][:, 0:512], wb[1][:, 0:512]]
        stat = u1[:, 8192:12288].bitcast(F32)[0:1, 0:2048]
        wmask4 = act[:, 0:2048]

        for s in ["dma", "lw", "lx", "mm", "mmb", "ev", "ev2", "ev3", "st", "st2",
                  "aux", "aux2", "aux3", "rdy", "fin"]:
            b.new_sem(s)

        x0r = ap["x0T"].rearrange("(k p) t -> p k t", p=128)
        wqkv0r = ap["wqkv0"].rearrange("(k p) c -> p k c", p=128)
        wproj0r = ap["wproj0"].rearrange("(k p) c -> p k c", p=128)
        gate0r = ap["gate0w"].rearrange("(k p) c -> p k c", p=128)
        up0r = ap["up0w"].rearrange("(k p) c -> p k c", p=128)
        down0r = ap["down0w"].rearrange("(k p) c -> p k c", p=128)
        xn0r = sc["xn0"].rearrange("(k p) t -> p k t", p=128)
        q0r = sc["q0"].rearrange("(k p) t -> p k t", p=128)
        k0r = sc["k0"].rearrange("(k p) t -> p k t", p=128)
        v0tr = sc["v0t"].rearrange("(k p) t -> p k t", p=128)
        fuser = sc["fuse"].rearrange("(k p) t -> p k t", p=128)
        attn0r = sc["attn0"].rearrange("(k p) t -> p k t", p=128)
        x1rr = sc["x1"].rearrange("(k p) t -> p k t", p=128)
        xn1br = sc["xn1b"].rearrange("(k p) t -> p k t", p=128)
        xn1fr = sc["xn1f"].rearrange("(k p) t -> p k t", p=128)

        def actv(k, t0, w):
            return act[:, T * k + t0: T * k + t0 + w]

        def act3(t0, w):
            return act[:].rearrange("p (k t) -> p k t", t=T)[:, :, t0:t0 + w]

        for dst, src in [(cmaskb[:], ap["dmaskb"]), (cident[:], ap["identb"]),
                         (conesc[:], ap["onescol"]), (conesb[:], ap["onescol_b"]),
                         (conesr[:], ap["onesrow"]), (ccfg[:], ap["cfg"]),
                         (colv[:, 0:48], ap["gs0col"]), (colv[:, 48:96], ap["us0col"]),
                         (colv[:, 96:112], ap["ds0col"])]:
            ldc = b.dma("sync", dst, src)
        for eng in ENGS:
            b.wait(eng, ldc)
        epsv = b.op("vector", lambda v: v.memset(colv[:, 151:152], 1e-5), "aux")
        b.wait("scalar", epsv)

        def emit_ln(src_r, xn_b_r, xn_f_r):
            NLB, LW = 16, 256
            xa_free = None
            for tb in range(NLB):
                b.wait("sync", xa_free)
                lx = b.dma("sync", xa[:].rearrange("p (k t) -> p k t", t=LW)
                           [:, :, 0:LW] if False else
                           xa[:, 0:NDT * LW].rearrange("p (k t) -> p k t", t=LW),
                           src_r[:, :, LW * tb:LW * (tb + 1)], "lx")
                b.wait("vector", lx)
                cast = b.op("vector", lambda v: v.tensor_copy(
                    wb2[0][:, 0:NDT * LW], xa[:, 0:NDT * LW]), "aux")
                b.wait("scalar", cast)
                sq = b.op("scalar", lambda s: s.activation(
                    wb2[1][:, 0:NDT * LW], wb2[0][:, 0:NDT * LW], AF.Square), "aux2")
                b.wait("tensor", cast)
                b.wait("tensor", sq)
                for k in range(NDT):
                    b.op("tensor", lambda t, k=k: t.matmul(
                        ps[6][0:1, 0:LW], conesb[:],
                        wb2[0][:, LW * k:LW * (k + 1)],
                        start=(k == 0), stop=(k == NDT - 1)))
                mmv = None
                for k in range(NDT):
                    mmv = b.op("tensor", lambda t, k=k: t.matmul(
                        ps[5][0:1, 0:LW], conesb[:],
                        wb2[1][:, LW * k:LW * (k + 1)],
                        start=(k == 0), stop=(k == NDT - 1)),
                        "mm" if k == NDT - 1 else None)
                b.wait("scalar", mmv)
                b.op("scalar", lambda s: s.mul(stat[:, 0:LW], ps[6][0:1, 0:LW],
                                               1.0 / 2048.0))
                st1 = b.op("scalar", lambda s: s.mul(stat[:, 256:256 + LW],
                                                     ps[5][0:1, 0:LW],
                                                     1.0 / 2048.0), "ev2")
                b.wait("vector", st1)
                msq = b.op("vector", lambda v: v.tensor_tensor(
                    stat[:, 512:512 + LW], stat[:, 0:LW], stat[:, 0:LW],
                    ALU.mult), "aux")
                b.wait("vector", msq)
                var = b.op("vector", lambda v: v.tensor_tensor(
                    stat[:, 768:768 + LW], stat[:, 256:256 + LW],
                    stat[:, 512:512 + LW], ALU.subtract), "aux3")
                b.wait("scalar", var)
                b.op("scalar", lambda s: s.activation(
                    stat[:, 1024:1024 + LW], stat[:, 768:768 + LW], AF.Ln,
                    bias=colv[0:1, 151:152]))
                rstd = b.op("scalar", lambda s: s.activation(
                    stat[:, 1280:1280 + LW], stat[:, 1024:1024 + LW], AF.Exp,
                    scale=-0.5), "rdy")
                b.wait("tensor", rstd)
                b.op("tensor", lambda t: t.matmul(
                    ps[5][:, 0:LW], conesr[:], stat[:, 0:LW],
                    start=True, stop=True))
                bc = b.op("tensor", lambda t: t.matmul(
                    ps[4][:, 0:LW], conesr[:], stat[:, 1280:1280 + LW],
                    start=True, stop=True), "mmb")
                b.wait("scalar", bc)
                b.op("scalar", lambda s: s.copy(mrow[:, 0:LW], ps[5][:, 0:LW]))
                mb = b.op("scalar", lambda s: s.copy(mrow[:, 256:256 + LW],
                                                     ps[4][:, 0:LW]), "rdy")
                b.wait("vector", mb)
                last = None
                for k in range(NDT):
                    b.op("vector", lambda v, k=k: v.tensor_tensor(
                        xa[:, LW * k:LW * (k + 1)], xa[:, LW * k:LW * (k + 1)],
                        mrow[:, 0:LW], ALU.subtract))
                    last = b.op("vector", lambda v, k=k, tb=tb: v.tensor_tensor(
                        actv(k, LW * tb, LW), xa[:, LW * k:LW * (k + 1)],
                        mrow[:, 256:256 + LW], ALU.mult),
                        "ev" if k == NDT - 1 else None)
                if xn_f_r is not None:
                    for k in range(NDT):
                        last = b.op("vector", lambda v, k=k: v.tensor_tensor(
                            xa[:, LW * k:LW * (k + 1)], xa[:, LW * k:LW * (k + 1)],
                            mrow[:, 256:256 + LW], ALU.mult),
                            "ev" if k == NDT - 1 else None)
                b.wait("gpsimd", last)
                stv = b.dma("gpsimd", xn_b_r[:, :, LW * tb:LW * (tb + 1)],
                            act3(LW * tb, LW), "st")
                if xn_f_r is not None:
                    stv = b.dma("gpsimd", xn_f_r[:, :, LW * tb:LW * (tb + 1)],
                                xa[:, 0:NDT * LW].rearrange("p (k t) -> p k t", t=LW),
                                "st2")
                xa_free = stv if xn_f_r is not None else last
            return ("ev", b.cnt["ev"]), ("st", b.cnt["st"])

        def emit_prequant():
            qf32 = u1[:, 0:8192].bitcast(F32)
            qf8 = u1[:, 8192:12288].bitcast(F8)
            srcs = [(ap["gate0w"], sc["g0q"], 48), (ap["up0w"], sc["u0q"], 48),
                    (ap["down0w"], sc["d0q"], 48)]
            slot_free = [None, None]
            for (wsrc, wdst, nrt) in srcs:
                sr = wsrc.rearrange("(k p) c -> p k c", p=128)
                dr = wdst.rearrange("(k p) c -> p k c", p=128)
                nk = wsrc.shape[0] // 128
                nc_ = wsrc.shape[1] // 2048
                for k in range(nk):
                    for cc in range(nc_):
                        sl = (k * nc_ + cc) % 2
                        b.wait("sync", slot_free[sl])
                        lq = b.dma("sync", qf32[:, 2048 * sl:2048 * (sl + 1)],
                                   sr[:, k, 2048 * cc:2048 * (cc + 1)], "lx")
                        b.wait("vector", lq)
                        c8 = b.op("vector", lambda v, sl=sl: v.tensor_copy(
                            qf8[:, 2048 * sl:2048 * (sl + 1)],
                            qf32[:, 2048 * sl:2048 * (sl + 1)]), "aux")
                        b.wait("vector", c8)
                        cv = b.op("vector", lambda v, sl=sl: v.tensor_copy(
                            wb2[sl][:, 0:2048],
                            qf8[:, 2048 * sl:2048 * (sl + 1)]), "ev")
                        b.wait("gpsimd", cv)
                        stq = b.dma("gpsimd", dr[:, k, 2048 * cc:2048 * (cc + 1)],
                                    wb2[sl][:, 0:2048], "st")
                        slot_free[sl] = stq
            return ("st", b.cnt["st"])

        pq_st = emit_prequant()
        b.wait("sync", pq_st)
        g0qr = sc["g0q"].rearrange("(k p) c -> p k c", p=128)
        u0qr = sc["u0q"].rearrange("(k p) c -> p k c", p=128)
        d0qr = sc["d0q"].rearrange("(k p) c -> p k c", p=128)

        ln0_ev, _ = emit_ln(x0r, xn0r, None)
        b.wait("tensor", ln0_ev)
        if upto <= 0:
            return _finalize(nc, b, e, dump_aps, out_final, fbv)

        def emit_fm_qk():
            w_free = [None, None]
            ev_marks = []
            st_marks = []
            for ct in range(32):
                sl = ct % 2
                b.wait("gpsimd", w_free[sl])
                lw = b.dma("gpsimd", wb[sl][:].rearrange("p (k c) -> p k c", c=128),
                           wqkv0r[:, :, 128 * ct:128 * (ct + 1)], "lw")
                b.wait("tensor", lw)
                mml = None
                for tb in range(8):
                    u = ct * 8 + tb
                    bank = ps[u % 3]
                    if u >= 3:
                        b.wait("tensor", ev_marks[u - 3])
                    for k in range(NDT):
                        mml = b.op("tensor", lambda t, sl=sl, k=k, tb=tb, bank=bank:
                                   t.matmul(bank[:],
                                            wb[sl][:, 128 * k:128 * (k + 1)],
                                            actv(k, 512 * tb, 512),
                                            start=(k == 0), stop=(k == NDT - 1)),
                                   "mm" if k == NDT - 1 else None)
                    obs = u % 4
                    b.wait("scalar", mml)
                    if u >= 4:
                        b.wait("scalar", st_marks[u - 4])
                    scale = SCL0 if ct < 16 else 1.0
                    evv = b.op("scalar", lambda s, bank=bank, obs=obs, scale=scale:
                               s.activation(ob[obs][:], bank[:], AF.Copy,
                                            scale=scale), "ev3")
                    ev_marks.append(evv)
                    b.wait("gpsimd", evv)
                    dv, kc = (q0r, ct) if ct < 16 else (k0r, ct - 16)
                    st_marks.append(b.dma(
                        "gpsimd", dv[:, kc, 512 * tb:512 * (tb + 1)], ob[obs][:],
                        "st"))
                w_free[sl] = mml

        emit_fm_qk()

        def emit_gateup():
            w_free = [None, None]
            ev_marks = []
            st_marks = []
            for ct in range(48):
                sl = ct % 2
                b.wait("gpsimd", w_free[sl])
                b.dma("gpsimd", wb[sl][:].rearrange("p (k c) -> p k c", c=128),
                      g0qr[:, :, 128 * ct:128 * (ct + 1)], "lw")
                lw = b.dma("gpsimd",
                           wb2[sl][:, 0:2048].rearrange("p (k c) -> p k c", c=128),
                           u0qr[:, :, 128 * ct:128 * (ct + 1)], "lw")
                b.wait("tensor", lw)
                mml = None
                for tb in range(8):
                    u = ct * 8 + tb
                    bg = ps[(2 * u) % 4]
                    bu = ps[(2 * u + 1) % 4]
                    if u >= 2:
                        b.wait("tensor", ev_marks[u - 2])
                    for k in range(NDT):
                        b.op("tensor", lambda t, sl=sl, k=k, tb=tb, bank=bg:
                             t.matmul(bank[:], wb[sl][:, 128 * k:128 * (k + 1)],
                                      actv(k, 512 * tb, 512),
                                      start=(k == 0), stop=(k == NDT - 1)))
                    for k in range(NDT):
                        mml = b.op("tensor", lambda t, sl=sl, k=k, tb=tb, bank=bu:
                                   t.matmul(bank[:],
                                            wb2[sl][:, 128 * k:128 * (k + 1)],
                                            actv(k, 512 * tb, 512),
                                            start=(k == 0), stop=(k == NDT - 1)),
                                   "mm" if k == NDT - 1 else None)
                    obs = u % 4
                    b.wait("scalar", mml)
                    if u >= 4:
                        b.wait("scalar", st_marks[u - 4])
                    sg = b.op("scalar", lambda s, bank=bg, obs=obs, ct=ct:
                              s.activation(ob[obs][:], bank[:], AF.Silu,
                                           scale=colv[:, ct:ct + 1]), "aux")
                    b.wait("vector", sg)
                    if u >= 4:
                        b.wait("vector", st_marks[u - 4])
                    fu = b.op("vector", lambda v, bank=bu, obs=obs, ct=ct:
                              v.scalar_tensor_tensor(
                                  ob[obs][:], bank[:], colv[:, 48 + ct:49 + ct],
                                  ob[obs][:], ALU.mult, ALU.mult), "ev3")
                    ev_marks.append(fu)
                    b.wait("gpsimd", fu)
                    st_marks.append(b.dma(
                        "gpsimd", fuser[:, ct, 512 * tb:512 * (tb + 1)], ob[obs][:],
                        "st"))
                w_free[sl] = mml

        emit_gateup()

        def emit_vtm():
            w_free = [None, None]
            ev_marks = []
            st_marks = []
            for vc in range(4):
                sl = vc % 2
                b.wait("gpsimd", w_free[sl])
                lw = b.dma("gpsimd",
                           wb2[sl][:].rearrange("p (k c) -> p k c", c=512),
                           wqkv0r[:, :, 4096 + 512 * vc:4096 + 512 * (vc + 1)], "lw")
                b.wait("tensor", lw)
                mml = None
                for tt in range(32):
                    u = vc * 32 + tt
                    bank = ps[u % 3]
                    if u >= 3:
                        b.wait("tensor", ev_marks[u - 3])
                    for k in range(NDT):
                        mml = b.op("tensor", lambda t, sl=sl, k=k, tt=tt, bank=bank:
                                   t.matmul(bank[:],
                                            actv(k, 128 * tt, 128),
                                            wb2[sl][:, 512 * k:512 * (k + 1)],
                                            start=(k == 0), stop=(k == NDT - 1)),
                                   "mm" if k == NDT - 1 else None)
                    obs = u % 4
                    b.wait("scalar", mml)
                    if u >= 4:
                        b.wait("scalar", st_marks[u - 4])
                    evv = b.op("scalar", lambda s, bank=bank, obs=obs:
                               s.copy(ob[obs][:], bank[:]), "ev3")
                    ev_marks.append(evv)
                    b.wait("gpsimd", evv)
                    st_marks.append(b.dma(
                        "gpsimd", v0tr[:, tt, 512 * vc:512 * (vc + 1)], ob[obs][:],
                        "st"))
                w_free[sl] = mml

        emit_vtm()
        p1_st = ("st", b.cnt["st"])
        if upto <= 1:
            return _finalize(nc, b, e, dump_aps, out_final, fbv)

        b.wait("vector", p1_st)
        for sub in range(4):
            for m1 in range(4):
                off = 512 * sub + 128 * m1
                if m1 < sub:
                    b.op("vector", lambda v, off=off: v.memset(
                        wmask4[:, off:off + 128], 0.0))
                elif m1 == sub:
                    b.op("vector", lambda v, off=off: v.tensor_copy(
                        wmask4[:, off:off + 128], cmaskb[:]))
                else:
                    b.op("vector", lambda v, off=off: v.memset(
                        wmask4[:, off:off + 128], -BIG))
        mask_rdy = b.op("vector", lambda v: v.tensor_copy(
            colv[:, 150:151], conesc[:]), "rdy")

        def emit_attn0():
            b.wait("sync", p1_st)
            b.wait("tensor", mask_rdy)
            plane_free = None
            for h in range(16):
                b.wait("sync", plane_free)
                b.dma("sync", wb2[0][:, 0:4096], q0r[:, h, :], "lx")
                b.dma("sync", wb2[0][:, 4096:8192], k0r[:, h, :], "lx")
                lx = b.dma("sync",
                           wb2[1][:, 0:4096].rearrange("p (tt c) -> p tt c", c=128),
                           v0tr[:, :, 128 * h:128 * (h + 1)], "lx")
                b.wait("tensor", lx)
                pv_g4 = [None, None]
                tr_last = None
                for qb in range(8):
                    for sub in range(4):
                        qt = 4 * qb + sub
                        L = 512 * (qb + 1)
                        b.wait("scalar", tr_last)
                        b.wait("vector", tr_last)
                        sev = []
                        for kt in range(qb + 1):
                            bank = ps[kt % 2]
                            if kt >= 2:
                                b.wait("tensor", sev[kt - 2])
                            smm = b.op("tensor", lambda t, qt=qt, kt=kt, bank=bank:
                                       t.matmul(bank[:],
                                                wb2[0][:, 128 * qt:128 * (qt + 1)],
                                                wb2[0][:, 4096 + 512 * kt:
                                                        4096 + 512 * (kt + 1)],
                                                start=True, stop=True), "mm")
                            if kt < qb:
                                b.wait("scalar", smm)
                                sev.append(b.op(
                                    "scalar", lambda s, kt=kt, bank=bank:
                                    s.copy(srow[:, 512 * kt:512 * (kt + 1)],
                                           bank[:]), "ev2"))
                            else:
                                b.wait("vector", smm)
                                sev.append(b.op(
                                    "vector", lambda v, kt=kt, bank=bank, sub=sub:
                                    v.scalar_tensor_tensor(
                                        srow[:, 512 * kt:512 * (kt + 1)], bank[:],
                                        0.0, wmask4[:, 512 * sub:512 * (sub + 1)],
                                        ALU.bypass, ALU.add), "ev2"))
                        if h == 0 and qt == 0:
                            b.wait("gpsimd", sev[-1])
                            b.dma("gpsimd", sc["dbg"][:, 0:512],
                                  srow[:, 0:512], "st2")
                        b.wait("vector", sev[-1])
                        if len(sev) >= 2:
                            b.wait("vector", sev[-2])
                        ngmax = b.op("vector", lambda v, L=L: v.tensor_reduce(
                            colv[:, 128:129], srow[:, 0:L], AX.X, ALU.max,
                            negate=True), "aux")
                        b.wait("scalar", ngmax)
                        b.wait("scalar", tr_last)
                        ex = b.op("scalar", lambda s, L=L: s.activation(
                            prow[:, 0:L], srow[:, 0:L], AF.Exp,
                            bias=colv[:, 128:129], accum_out=colv[:, 129:130]),
                            "aux2")
                        b.wait("vector", ex)
                        rcp = b.op("vector", lambda v: v.reciprocal(
                            colv[:, 130:131], colv[:, 129:130]), "aux")
                        b.wait("vector", rcp)
                        pn = b.op("vector", lambda v, L=L: v.tensor_scalar(
                            prown[:, 0:L], prow[:, 0:L], colv[:, 130:131], 0.0,
                            ALU.mult, ALU.add), "aux3")
                        if h == 0 and qt == 0:
                            b.wait("gpsimd", pn)
                            b.dma("gpsimd", sc["dbg"][:, 512:515],
                                  colv[:, 128:131], "st2")
                            dcp = b.op("vector", lambda v: v.tensor_copy(
                                fbv[1][:], prown[:, 0:512]), "aux")
                            b.wait("gpsimd", dcp)
                            b.dma("gpsimd", sc["dbg"][:, 1024:1536],
                                  fbv[1][:], "st2")
                        b.wait("tensor", pn)
                        nkb = 4 * (qb + 1)
                        pv_last = None
                        for g4 in range((nkb + 3) // 4):
                            grp = min(4, nkb - 4 * g4)
                            gsl = g4 % 2
                            trl = None
                            for g in range(grp):
                                kb = 4 * g4 + g
                                trl = b.op("tensor", lambda t, kb=kb, g=g:
                                           t.transpose(
                                               psb[:, 128 * g:128 * (g + 1)],
                                               prown[:, 128 * kb:128 * (kb + 1)],
                                               cident[:]),
                                           "mmb" if g == grp - 1 else None)
                            tr_last = trl
                            b.wait("scalar", trl)
                            b.wait("scalar", pv_g4[gsl])
                            pt = b.op("scalar", lambda s, grp=grp, gsl=gsl:
                                      s.copy(ptbv[gsl][:, 0:128 * grp],
                                             psb[:, 0:128 * grp]), "ev3")
                            b.wait("tensor", pt)
                            for g in range(grp):
                                kb = 4 * g4 + g
                                pv_last = b.op(
                                    "tensor", lambda t, kb=kb, g=g, gsl=gsl, sub=sub:
                                    t.matmul(
                                        ps[5][:, 128 * sub:128 * (sub + 1)],
                                        wb2[1][:, 128 * kb:128 * (kb + 1)],
                                        ptbv[gsl][:, 128 * g:128 * (g + 1)],
                                        start=(kb == 0), stop=(kb == nkb - 1)),
                                    "mm" if kb == nkb - 1 else None)
                            pv_g4[gsl] = pv_last
                    b.wait("scalar", pv_last)
                    obs = qb % 2
                    evv = b.op("scalar", lambda s, obs=obs: s.copy(
                        ob[obs][:], ps[5][:]), "ev3")
                    b.wait("gpsimd", evv)
                    stv = b.dma("gpsimd", attn0r[:, h, 512 * qb:512 * (qb + 1)],
                                ob[obs][:], "st")
                    if qb >= 1:
                        pass
                    b.wait("tensor", evv)
                plane_free = stv
            return stv

        emit_attn0()
        p2_st = ("st", b.cnt["st"])
        if upto <= 2:
            return _finalize(nc, b, e, dump_aps, out_final, fbv)

        def emit_p3():
            b.wait("gpsimd", p2_st)
            b.wait("sync", p2_st)
            wdn = act[:, 0:48 * 512]
            wpj = act[:, 48 * 512:64 * 512]
            chunk_used = [None, None]
            down_free = [None] * 4
            slot_free = [None, None]
            proj_free = [None, None]
            last_store = None
            for ocg in range(4):
                if last_store is not None:
                    b.wait("gpsimd", last_store)
                b.dma("gpsimd", wdn.rearrange("p (k c) -> p k c", c=512),
                      d0qr[:, :, 512 * ocg:512 * (ocg + 1)], "lw")
                lwp = b.dma("gpsimd", wpj.rearrange("p (k c) -> p k c", c=512),
                            wproj0r[:, :, 512 * ocg:512 * (ocg + 1)], "lw")
                b.wait("tensor", lwp)
                for tb in range(8):
                    for kc in range(4):
                        sl = kc % 2
                        b.wait("sync", chunk_used[sl])
                        if kc < 3:
                            lxv = b.dma("sync",
                                        wb2[sl][:].rearrange("p (k t) -> p k t",
                                                             t=512),
                                        fuser[:, 16 * kc:16 * (kc + 1),
                                              512 * tb:512 * (tb + 1)], "lx")
                        else:
                            lxv = b.dma("sync",
                                        wb2[sl][:].rearrange("p (k t) -> p k t",
                                                             t=512),
                                        attn0r[:, :, 512 * tb:512 * (tb + 1)],
                                        "lx")
                        b.wait("tensor", lxv)
                        mml = None
                        for oct_ in range(4):
                            if kc < 3:
                                bank = ps[oct_]
                                if kc == 0:
                                    b.wait("tensor", down_free[oct_])
                                for k in range(16):
                                    kk = 16 * kc + k
                                    mml = b.op(
                                        "tensor",
                                        lambda t, sl=sl, k=k, kk=kk, oct_=oct_,
                                        bank=bank:
                                        t.matmul(
                                            bank[:],
                                            wdn[:, 512 * kk + 128 * oct_:
                                                512 * kk + 128 * (oct_ + 1)],
                                            wb2[sl][:, 512 * k:512 * (k + 1)],
                                            start=(kk == 0), stop=(kk == 47)),
                                        "mm" if k == 15 else None)
                            else:
                                pj = 4 + (oct_ % 2)
                                bank = ps[pj]
                                b.wait("tensor", proj_free[oct_ % 2])
                                for k in range(16):
                                    mml = b.op(
                                        "tensor",
                                        lambda t, sl=sl, k=k, oct_=oct_, bank=bank:
                                        t.matmul(
                                            bank[:],
                                            wpj[:, 512 * k + 128 * oct_:
                                                512 * k + 128 * (oct_ + 1)],
                                            wb2[sl][:, 512 * k:512 * (k + 1)],
                                            start=(k == 0), stop=(k == 15)),
                                        "mm" if k == 15 else None)
                                fs = oct_ % 2
                                b.wait("scalar", mml)
                                b.wait("scalar", slot_free[fs])
                                dse = b.op(
                                    "scalar",
                                    lambda s, oct_=oct_, ocg=ocg, fs=fs:
                                    s.activation(
                                        fbv[fs][:], ps[oct_][:], AF.Copy,
                                        scale=colv[:, 96 + 4 * ocg + oct_:
                                                   97 + 4 * ocg + oct_]), "aux")
                                down_free[oct_] = dse
                                b.wait("sync", slot_free[fs])
                                lx0 = b.dma("sync", fbv_x[fs][:],
                                            x0r[:, 4 * ocg + oct_,
                                                512 * tb:512 * (tb + 1)], "lx")
                                b.wait("vector", dse)
                                b.wait("vector", lx0)
                                sm = b.op("vector",
                                          lambda v, bank=bank, fs=fs:
                                          v.scalar_tensor_tensor(
                                              fbv[fs][:], bank[:], 0.0,
                                              fbv[fs][:], ALU.bypass, ALU.add),
                                          "aux")
                                b.wait("vector", sm)
                                ad = b.op("vector",
                                          lambda v, fs=fs:
                                          v.tensor_tensor(
                                              fbv[fs][:], fbv[fs][:],
                                              fbv_x[fs][:], ALU.add), "ev")
                                proj_free[oct_ % 2] = ad
                                b.wait("gpsimd", ad)
                                last_store = b.dma(
                                    "gpsimd",
                                    x1rr[:, 4 * ocg + oct_,
                                         512 * tb:512 * (tb + 1)],
                                    fbv[fs][:], "st2")
                                slot_free[fs] = last_store
                        chunk_used[sl] = mml
            return last_store

        fbv_x = [act[:, 32768:33792].bitcast(F32), act[:, 33792:34816].bitcast(F32)]
        p3_last = emit_p3()
        if upto <= 3:
            return _finalize(nc, b, e, dump_aps, out_final, fbv)

        b.wait("sync", p3_last)
        ln1_ev, ln1_st = emit_ln(x1rr, xn1br, xn1fr)
        b.wait("tensor", ln1_ev)

        wqkv1r = ap["wqkv1T"].rearrange("(k p) c -> p k c", p=128)
        q1r = sc["q1"].rearrange("(k p) t -> p k t", p=128)
        k1r = sc["k1"].rearrange("(k p) t -> p k t", p=128)
        v1r = sc["v1"].rearrange("(k p) t -> p k t", p=128)
        aqr = sc["aq"].rearrange("(k p) t -> p k t", p=128)
        akpr = sc["akp"].rearrange("(k p) t -> p k t", p=128)
        v1tpr = sc["v1tp"].rearrange("(tt p) c -> p tt c", p=128)
        fqr = sc["fq"].rearrange("(k p) t -> p k t", p=128)
        fkr = sc["fk"].rearrange("(k p) t -> p k t", p=128)
        fvr = sc["fv"].rearrange("(k p) t -> p k t", p=128)
        lrr = sc["lr"].rearrange("(tt p) c -> p tt c", p=128)

        def emit_qkv1():
            w_free = [None, None]
            ev_marks = []
            st_marks = []
            for ct in range(48):
                sl = ct % 2
                b.wait("gpsimd", w_free[sl])
                lw = b.dma("gpsimd", wb[sl][:].rearrange("p (k c) -> p k c", c=128),
                           wqkv1r[:, :, 128 * ct:128 * (ct + 1)], "lw")
                b.wait("tensor", lw)
                mml = None
                for tb in range(8):
                    u = ct * 8 + tb
                    bank = ps[u % 3]
                    if u >= 3:
                        b.wait("tensor", ev_marks[u - 3])
                    for k in range(NDT):
                        mml = b.op("tensor", lambda t, sl=sl, k=k, tb=tb, bank=bank:
                                   t.matmul(bank[:],
                                            wb[sl][:, 128 * k:128 * (k + 1)],
                                            actv(k, 512 * tb, 512),
                                            start=(k == 0), stop=(k == NDT - 1)),
                                   "mm" if k == NDT - 1 else None)
                    obs = u % 4
                    b.wait("scalar", mml)
                    if u >= 4:
                        b.wait("scalar", st_marks[u - 4])
                    evv = b.op("scalar", lambda s, bank=bank, obs=obs:
                               s.copy(ob[obs][:], bank[:]), "ev3")
                    ev_marks.append(evv)
                    b.wait("gpsimd", evv)
                    if ct < 16:
                        dv, kc = q1r, ct
                    elif ct < 32:
                        dv, kc = k1r, ct - 16
                    else:
                        dv, kc = v1r, ct - 32
                    st_marks.append(b.dma(
                        "gpsimd", dv[:, kc, 512 * tb:512 * (tb + 1)], ob[obs][:],
                        "st"))
                w_free[sl] = mml

            zv = b.op("vector", lambda v: v.memset(ob[0][:], 0.0), "aux")
            b.wait("gpsimd", zv)
            padst = None
            for pt in range(8):
                for j in range(4):
                    padst = b.dma("gpsimd",
                                  v1tpr[:, pt, 512 * j:512 * (j + 1)],
                                  ob[0][:], "st")
            w_free2 = [None, None]
            ev2_marks = []
            st2_marks = []
            for vc in range(4):
                sl = vc % 2
                b.wait("gpsimd", w_free2[sl])
                lw = b.dma("gpsimd",
                           wb2[sl][:].rearrange("p (k c) -> p k c", c=512),
                           wqkv1r[:, :, 4096 + 512 * vc:4096 + 512 * (vc + 1)],
                           "lw")
                b.wait("tensor", lw)
                mml = None
                for tt in range(32):
                    u = vc * 32 + tt
                    bank = ps[u % 3]
                    if u >= 3:
                        b.wait("tensor", ev2_marks[u - 3])
                    for k in range(NDT):
                        mml = b.op("tensor", lambda t, sl=sl, k=k, tt=tt, bank=bank:
                                   t.matmul(bank[:],
                                            actv(k, 128 * tt, 128),
                                            wb2[sl][:, 512 * k:512 * (k + 1)],
                                            start=(k == 0), stop=(k == NDT - 1)),
                                   "mm" if k == NDT - 1 else None)
                    obs = u % 4
                    b.wait("scalar", mml)
                    if u >= 4:
                        b.wait("scalar", st2_marks[u - 4])
                    evv = b.op("scalar", lambda s, bank=bank, obs=obs:
                               s.copy(ob[obs][:], bank[:]), "ev3")
                    ev2_marks.append(evv)
                    b.wait("gpsimd", evv)
                    st2_marks.append(b.dma(
                        "gpsimd", v1tpr[:, 8 + tt, 512 * vc:512 * (vc + 1)],
                        ob[obs][:], "st"))
                w_free2[sl] = mml
            return ("st", b.cnt["st"])

        qkv1_st = emit_qkv1()

        for dst, src in [(colv[:, 112:128], ap["qnormcol"]),
                         (colv[:, 0:16], ap["knormcol"]),
                         (colv[:, 16:32], ap["qscol"]), (colv[:, 32:48], ap["kscol"]),
                         (colv[:, 48:64], ap["qocol"]), (colv[:, 64:80], ap["kocol"])]:
            ldc2 = b.dma("sync", dst, src)

        fb2 = wb[0][:, 0:1024].bitcast(F32)
        fb3 = wb[0][:, 1024:2048].bitcast(F32)
        e6 = b.op("vector", lambda v: v.memset(colv[:, 152:153], 1e-6), "aux")
        e12 = b.op("vector", lambda v: v.memset(colv[:, 153:154], 1e-12), "aux")
        lrwb = e(nc.sbuf_tensor("lrwb", [128, 192], BF16))
        lrbb = e(nc.sbuf_tensor("lrbb", [128, 12], F32))
        llw = b.dma("gpsimd", lrwb[:].rearrange("p (k c) -> p k c", c=12),
                    ap["lrwT"].rearrange("(k p) c -> p k c", p=128), "lw")
        llb = b.dma("sync", lrbb[:], ap["lrb"], "lx")

        def emit_prep_qkv(which):
            qb_ = u1[:, 0:8192]
            is_q = which == "q"
            srcr = q1r if is_q else k1r
            dst_ar = aqr if is_q else akpr
            dst_fr = fqr if is_q else fkr
            nw_c, sc_c, of_c = ((112, 16, 48) if is_q else (0, 32, 64))
            ld_free = ("st", b.cnt["st"])
            for tb in range(8):
                b.wait("sync", ld_free)
                lq = b.dma("sync", qb_.rearrange("p (k t) -> p k t", t=512),
                           srcr[:, :, 512 * tb:512 * (tb + 1)], "lx")
                b.dma("sync", fbv[0][0:64, :],
                      ap["cosfm"][:, 512 * tb:512 * (tb + 1)], "lx")
                b.dma("sync", fbv[0][64:128, :],
                      ap["cosfm"][:, 512 * tb:512 * (tb + 1)], "lx")
                b.dma("sync", fbv[1][0:64, :],
                      ap["sinfm"][:, 512 * tb:512 * (tb + 1)], "lx")
                lcs2 = b.dma("sync", fbv[1][64:128, :],
                             ap["sinfm"][:, 512 * tb:512 * (tb + 1)], "lx")
                b.wait("scalar", lq)
                sq = b.op("scalar", lambda s: s.activation(
                    wb2[1][:], qb_, AF.Square), "aux2")
                b.wait("tensor", sq)
                mmv = None
                for k in range(NDT):
                    mmv = b.op("tensor", lambda t, k=k: t.matmul(
                        ps[6][0:1, 0:512], conesb[:],
                        wb2[1][:, 512 * k:512 * (k + 1)],
                        start=(k == 0), stop=(k == NDT - 1)),
                        "mm" if k == NDT - 1 else None)
                b.wait("scalar", mmv)
                b.op("scalar", lambda s: s.mul(stat[:, 0:512], ps[6][0:1, 0:512],
                                               1.0 / 2048.0))
                l1 = b.op("scalar", lambda s: s.activation(
                    stat[:, 512:1024], stat[:, 0:512], AF.Ln,
                    bias=colv[0:1, 152:153]), "aux2")
                b.wait("scalar", l1)
                rs1 = b.op("scalar", lambda s: s.activation(
                    stat[:, 0:512], stat[:, 512:1024], AF.Exp, scale=-0.5), "rdy")
                if is_q:
                    b.wait("scalar", rs1)
                    rs1 = b.op("scalar", lambda s: s.mul(
                        stat[:, 0:512], stat[:, 0:512], SCL0), "rdy")
                b.wait("tensor", rs1)
                bc = b.op("tensor", lambda t: t.matmul(
                    ps[5][:, 0:512], conesr[:], stat[:, 0:512],
                    start=True, stop=True), "mmb")
                b.wait("scalar", bc)
                mb = b.op("scalar", lambda s: s.copy(mrow[:, 0:512],
                                                     ps[5][:, 0:512]), "rdy")
                sl1 = b.op("scalar", lambda s: s.activation(
                    wb2[0][:], qb_, AF.Silu), "aux2")
                b.wait("vector", sl1)
                fpl = None
                for k in range(NDT):
                    fpl = b.op("vector", lambda v, k=k: v.tensor_scalar(
                        wb2[1][:, 512 * k:512 * (k + 1)],
                        wb2[0][:, 512 * k:512 * (k + 1)],
                        colv[:, sc_c + k:sc_c + k + 1],
                        colv[:, of_c + k:of_c + k + 1],
                        ALU.mult, ALU.add), "ev2" if k == NDT - 1 else None)
                b.wait("vector", mb)
                aql = None
                for k in range(NDT):
                    b.op("vector", lambda v, k=k: v.tensor_tensor(
                        wb2[0][:, 512 * k:512 * (k + 1)],
                        qb_[:, 512 * k:512 * (k + 1)], mrow[:, 0:512], ALU.mult))
                for k in range(NDT):
                    aql = b.op("vector", lambda v, k=k: v.tensor_scalar(
                        qb_[:, 512 * k:512 * (k + 1)],
                        wb2[0][:, 512 * k:512 * (k + 1)],
                        colv[:, nw_c + k:nw_c + k + 1], 0.0,
                        ALU.mult, ALU.add), "ev2" if k == NDT - 1 else None)
                b.wait("vector", lcs2)
                b.wait("vector", aql)
                rpl = None
                for k in range(NDT):
                    o = 512 * k
                    b.wait("vector", rpl)
                    b.op("vector", lambda v, o=o: v.tensor_tensor(
                        fb2[0:64, :], qb_[0:64, o:o + 512], fbv[0][0:64, :],
                        ALU.mult))
                    t2 = b.op("vector", lambda v, o=o: v.tensor_tensor(
                        fb3[0:64, :], qb_[64:128, o:o + 512], fbv[1][64:128, :],
                        ALU.mult), "aux3")
                    b.wait("vector", t2)
                    b.op("vector", lambda v, o=o: v.tensor_tensor(
                        wb2[0][0:64, o:o + 512], fb2[0:64, :], fb3[0:64, :],
                        ALU.subtract))
                    b.op("vector", lambda v, o=o: v.tensor_tensor(
                        fb2[64:128, :], qb_[64:128, o:o + 512],
                        fbv[0][64:128, :], ALU.mult))
                    t4 = b.op("vector", lambda v, o=o: v.tensor_tensor(
                        fb3[64:128, :], qb_[0:64, o:o + 512], fbv[1][0:64, :],
                        ALU.mult), "aux3")
                    b.wait("vector", t4)
                    rpl = b.op("vector", lambda v, o=o: v.tensor_tensor(
                        wb2[0][64:128, o:o + 512], fb2[64:128, :],
                        fb3[64:128, :], ALU.add), "ev2")
                b.wait("gpsimd", rpl)
                if is_q:
                    sta = b.dma("gpsimd", dst_ar[:, :, 512 * tb:512 * (tb + 1)],
                                wb2[0][:].rearrange("p (k t) -> p k t", t=512),
                                "st")
                else:
                    sta = b.dma("gpsimd",
                                dst_ar[:, :, 1024 + 512 * tb:1024 + 512 * (tb + 1)],
                                wb2[0][:].rearrange("p (k t) -> p k t", t=512),
                                "st")
                b.wait("scalar", fpl)
                b.wait("scalar", rpl)
                sq2 = b.op("scalar", lambda s: s.activation(
                    qb_, wb2[1][:], AF.Square), "aux2")
                fst = None
                for hh in range(4):
                    b.wait("tensor", sq2)
                    mmv = None
                    for k in range(4):
                        mmv = b.op("tensor", lambda t, hh=hh, k=k: t.matmul(
                            ps[6][0:1, 0:512], conesb[:],
                            qb_[:, 512 * (4 * hh + k):512 * (4 * hh + k + 1)],
                            start=(k == 0), stop=(k == 3)),
                            "mm" if k == 3 else None)
                    b.wait("scalar", mmv)
                    lnn = b.op("scalar", lambda s: s.activation(
                        stat[:, 0:512], ps[6][0:1, 0:512], AF.Ln), "aux2")
                    b.wait("scalar", lnn)
                    ex2 = b.op("scalar", lambda s: s.activation(
                        stat[:, 512:1024], stat[:, 0:512], AF.Exp, scale=0.5),
                        "rdy")
                    b.wait("vector", ex2)
                    ad2 = b.op("vector", lambda v: v.tensor_scalar(
                        stat[:, 0:512], stat[:, 512:1024],
                        colv[0:1, 153:154], 0.0, ALU.add, ALU.add), "aux3")
                    b.wait("vector", ad2)
                    rc2 = b.op("vector", lambda v: v.reciprocal(
                        stat[:, 512:1024], stat[:, 0:512]), "rdy")
                    b.wait("tensor", rc2)
                    bc2 = b.op("tensor", lambda t: t.matmul(
                        ps[5][:, 0:512], conesr[:], stat[:, 512:1024],
                        start=True, stop=True), "mmb")
                    b.wait("scalar", bc2)
                    mb2 = b.op("scalar", lambda s: s.copy(
                        mrow[:, 0:512], ps[5][:, 0:512]), "rdy")
                    b.wait("vector", mb2)
                    fnl = None
                    for k in range(4):
                        kk = 4 * hh + k
                        fnl = b.op("vector", lambda v, kk=kk: v.tensor_tensor(
                            wb2[0][:, 512 * kk:512 * (kk + 1)],
                            wb2[1][:, 512 * kk:512 * (kk + 1)],
                            mrow[:, 0:512], ALU.mult),
                            "ev2" if k == 3 else None)
                    b.wait("gpsimd", fnl)
                    for k in range(4):
                        kk = 4 * hh + k
                        fst = b.dma("gpsimd",
                                    dst_fr[:, kk, 512 * tb:512 * (tb + 1)],
                                    wb2[0][:, 512 * kk:512 * (kk + 1)], "st")
                ld_free = fst
            return ("st", b.cnt["st"])

        b.wait("sync", qkv1_st)
        for eng in ("vector", "scalar", "tensor"):
            b.wait(eng, ldc2)
            b.wait(eng, llb)
        s_fix = emit_prep_qkv("q")
        s_fix = emit_prep_qkv("k")
        b.wait("sync", s_fix)

        zv2 = b.op("vector", lambda v: v.memset(ob[0][:], 0.0), "aux")
        b.wait("gpsimd", zv2)
        for k in range(NDT):
            for j in range(2):
                b.dma("gpsimd", akpr[:, k, 512 * j:512 * (j + 1)], ob[0][:], "st")

        def emit_fv():
            ld_free = None
            for tb in range(8):
                b.wait("sync", ld_free)
                lv = b.dma("sync", u1[:, 0:8192].rearrange("p (k t) -> p k t",
                                                           t=512),
                           v1r[:, :, 512 * tb:512 * (tb + 1)], "lx")
                b.wait("scalar", lv)
                sv = b.op("scalar", lambda s: s.activation(
                    wb2[0][:], u1[:, 0:8192], AF.Silu), "ev2")
                b.wait("gpsimd", sv)
                stv = b.dma("gpsimd", fvr[:, :, 512 * tb:512 * (tb + 1)],
                            wb2[0][:].rearrange("p (k t) -> p k t", t=512), "st")
                ld_free = sv
            return stv

        emit_fv()

        def emit_lr():
            b.wait("tensor", llw)
            prev = None
            for tt in range(32):
                bank = ps[tt % 2]
                if tt >= 2:
                    b.wait("tensor", lr_ev[tt - 2])
                mmv = None
                for k in range(NDT):
                    mmv = b.op("tensor", lambda t, tt=tt, k=k, bank=bank: t.matmul(
                        bank[:, 0:12], actv(k, 128 * tt, 128),
                        lrwb[:, 12 * k:12 * (k + 1)],
                        start=(k == 0), stop=(k == NDT - 1)),
                        "mm" if k == NDT - 1 else None)
                b.wait("vector", mmv)
                b.wait("vector", llb)
                z0 = b.op("vector", lambda v, bank=bank: v.tensor_tensor(
                    fb2[:, 0:12], bank[:, 0:12], lrbb[:], ALU.add), "aux3")
                b.wait("vector", z0)
                b.wait("vector", lr_sc[-1] if lr_sc else None)
                z1 = b.op("vector", lambda v: v.tensor_scalar(
                    fb2[:, 16:28], fb2[:, 0:12], 1.0, BASE_LR_INV,
                    ALU.mult, ALU.add), "aux3")
                lr_ev.append(z0)
                b.wait("scalar", z1)
                b.wait("scalar", lr_fin[-1] if lr_fin else None)
                ab1 = b.op("scalar", lambda s: s.activation(
                    fb3[:, 0:12], fb2[:, 16:28], AF.Abs), "aux2")
                b.wait("scalar", ab1)
                ex1 = b.op("scalar", lambda s: s.activation(
                    fb3[:, 16:28], fb3[:, 0:12], AF.Exp, scale=-1.0), "aux2")
                b.wait("scalar", ex1)
                ln1_ = b.op("scalar", lambda s: s.activation(
                    fb3[:, 32:44], fb3[:, 16:28], AF.Ln,
                    bias=conesc[:]), "aux2")
                rl1 = b.op("scalar", lambda s: s.activation(
                    fb3[:, 48:60], fb2[:, 16:28], AF.Relu), "aux2")
                lr_sc.append(rl1)
                b.wait("vector", rl1)
                b.wait("vector", prev)
                fin1 = b.op("vector", lambda v: v.tensor_tensor(
                    fb2[:, 32:44], fb3[:, 32:44], fb3[:, 48:60],
                    ALU.add), "ev2")
                lr_fin.append(fin1)
                b.wait("gpsimd", fin1)
                prev = b.dma("gpsimd", lrr[:, tt, 0:12], fb2[:, 32:44], "st")
            return prev

        lr_ev = []
        lr_sc = []
        lr_fin = []
        p4_last = emit_lr()
        p4_st = ("st", b.cnt["st"])
        if upto <= 5:
            return _finalize(nc, b, e, dump_aps, out_final, fbv)

        attn1r = sc["attn1"].rearrange("(k p) t -> p k t", p=128)
        acb = e(nc.sbuf_tensor("acb", [128, 192], F32))
        bcb = e(nc.sbuf_tensor("bcb", [128, 192], F32))
        crow = e(nc.sbuf_tensor("crow", [1, 192], F32))
        la1 = b.dma("sync", crow[:], ap["wacoef"], "lx")
        b.wait("tensor", la1)
        bca = b.op("tensor", lambda t: t.matmul(
            ps[5][:, 0:192], conesr[:], crow[:], start=True, stop=True), "mmb")
        b.wait("scalar", bca)
        eva = b.op("scalar", lambda s: s.copy(acb[:], ps[5][:, 0:192]), "rdy")
        b.wait("sync", eva)
        lb1 = b.dma("sync", crow[:], ap["wbcoef"], "lx")
        b.wait("tensor", lb1)
        bcb_ = b.op("tensor", lambda t: t.matmul(
            ps[5][:, 0:192], conesr[:], crow[:], start=True, stop=True), "mmb")
        b.wait("scalar", bcb_)
        evb = b.op("scalar", lambda s: s.copy(bcb[:], ps[5][:, 0:192]), "rdy")

        srow5 = u1[:, 0:3072].bitcast(F32)
        prow5 = u1[:, 3072:4608]
        prown5 = u1[:, 4608:6144]
        mbuf = u1[:, 6144:12288]

        def emit_attn1():
            roff = {}

            def get_roff(eng_name):
                if eng_name not in roff:
                    def fn(E):
                        r = E.alloc_register("roffr_" + eng_name)
                        E.reg_load(r, ccfg[0:1, 0:1])
                        roff[eng_name + "_v"] = E.snap(r, min_val=0, max_val=2048)
                        r2 = E.alloc_register("rttr_" + eng_name)
                        E.reg_load(r2, ccfg[0:1, 1:2])
                        roff[eng_name + "_t"] = E.snap(r2, min_val=0, max_val=16)
                    b.add(eng_name, fn)
                    roff[eng_name] = True
                return (lambda: roff[eng_name + "_v"],
                        lambda: roff[eng_name + "_t"])

            rsync, rsync_t = get_roff("sync")
            b.wait("sync", p4_st)
            akcr = sc["akc"].rearrange("(k p) t -> p k t", p=128)
            aqor = sc["aqo"].rearrange("(k p) t -> p k t", p=128)
            vcr = sc["vc"].rearrange("(tt p) c -> p tt c", p=128)
            b.add("sync", lambda E: E.dma_start(
                out=akcr, in_=akpr[:, :, bass.ds(rsync(), 3072)]
            ).then_inc(b.sem["st2"], 16))
            b.cnt["st2"] += 16
            b.add("sync", lambda E: E.dma_start(
                out=aqor, in_=aqr[:, :, bass.ds(rsync(), 2048)]
            ).then_inc(b.sem["st2"], 16))
            b.cnt["st2"] += 16
            b.add("sync", lambda E: E.dma_start(
                out=vcr, in_=v1tpr[:, bass.ds(rsync_t(), 24), :]
            ).then_inc(b.sem["st2"], 16))
            b.cnt["st2"] += 16
            stage_done = ("st2", b.cnt["st2"])
            b.wait("sync", stage_done)
            b.wait("vector", evb)
            plane_free = None
            evv = None
            ob_busy5 = [None, None]
            for qb in range(4):
                mlast = None
                for sub in range(4):
                    for kb in range(12):
                        idx = qb * 48 + sub * 12 + kb
                        off = 1536 * sub + 128 * kb
                        mlast = b.op("vector", lambda v, idx=idx, off=off:
                                     v.tensor_scalar(
                                         mbuf[:, off:off + 128], cmaskb[:],
                                         acb[:, idx:idx + 1], bcb[:, idx:idx + 1],
                                         ALU.mult, ALU.add),
                                     "aux3" if (sub == 3 and kb == 11) else None)
                for h in range(16):
                    b.wait("sync", plane_free)
                    b.dma("sync", wb2[0][:, 0:1536],
                          akcr[:, h, 512 * qb:512 * qb + 1536], "lx")
                    b.dma("sync", wb2[0][:, 1536:2048],
                          aqor[:, h, 512 * qb:512 * (qb + 1)], "lx")
                    lx5 = b.dma(
                        "sync",
                        wb2[1][:, 0:1536].rearrange("p (tt c) -> p tt c", c=128),
                        vcr[:, 4 * qb:4 * qb + 12, 128 * h:128 * (h + 1)], "lx")
                    b.wait("tensor", lx5)
                    if h == 0 and qb == 0:
                        b.wait("tensor", mlast)
                    pv_g4 = [None, None]
                    tr_last = None
                    for sub in range(4):
                        sev = []
                        for kt in range(3):
                            bank = ps[kt % 2]
                            if kt >= 2:
                                b.wait("tensor", sev[kt - 2])
                            smm = b.op("tensor", lambda t, sub=sub, kt=kt, bank=bank:
                                       t.matmul(bank[:],
                                                wb2[0][:, 1536 + 128 * sub:
                                                        1536 + 128 * (sub + 1)],
                                                wb2[0][:, 512 * kt:512 * (kt + 1)],
                                                start=True, stop=True), "mm")
                            b.wait("vector", smm)
                            sev.append(b.op(
                                "vector", lambda v, sub=sub, kt=kt, bank=bank:
                                v.scalar_tensor_tensor(
                                    srow5[:, 512 * kt:512 * (kt + 1)], bank[:],
                                    0.0,
                                    mbuf[:, 1536 * sub + 512 * kt:
                                         1536 * sub + 512 * (kt + 1)],
                                    ALU.bypass, ALU.add), "ev2"))
                        b.wait("vector", sev[-1])
                        ngmax = b.op("vector", lambda v: v.tensor_reduce(
                            colv[:, 128:129], srow5[:, 0:1536], AX.X, ALU.max,
                            negate=True), "aux")
                        b.wait("scalar", ngmax)
                        b.wait("scalar", tr_last)
                        ex = b.op("scalar", lambda s: s.activation(
                            prow5[:, 0:1536], srow5[:, 0:1536], AF.Exp,
                            bias=colv[:, 128:129], accum_out=colv[:, 129:130]),
                            "aux2")
                        b.wait("vector", ex)
                        rcp = b.op("vector", lambda v: v.reciprocal(
                            colv[:, 130:131], colv[:, 129:130]), "aux")
                        b.wait("vector", rcp)
                        b.wait("vector", tr_last)
                        pn = b.op("vector", lambda v: v.tensor_scalar(
                            prown5[:, 0:1536], prow5[:, 0:1536],
                            colv[:, 130:131], 0.0, ALU.mult, ALU.add), "aux3")
                        b.wait("tensor", pn)
                        pv_last = None
                        for g4 in range(3):
                            gsl = g4 % 2
                            trl = None
                            for g in range(4):
                                kb = 4 * g4 + g
                                trl = b.op("tensor", lambda t, kb=kb, g=g:
                                           t.transpose(
                                               psb[:, 128 * g:128 * (g + 1)],
                                               prown5[:, 128 * kb:128 * (kb + 1)],
                                               cident[:]),
                                           "mmb" if g == 3 else None)
                            tr_last = trl
                            b.wait("scalar", trl)
                            b.wait("scalar", pv_g4[gsl])
                            pt = b.op("scalar", lambda s, gsl=gsl:
                                      s.copy(ptbv[gsl][:, 0:512],
                                             psb[:, 0:512]), "ev3")
                            b.wait("tensor", pt)
                            for g in range(4):
                                kb = 4 * g4 + g
                                pv_last = b.op(
                                    "tensor", lambda t, kb=kb, g=g, gsl=gsl,
                                    sub=sub:
                                    t.matmul(
                                        ps[5][:, 128 * sub:128 * (sub + 1)],
                                        wb2[1][:, 128 * kb:128 * (kb + 1)],
                                        ptbv[gsl][:, 128 * g:128 * (g + 1)],
                                        start=(kb == 0), stop=(kb == 11)),
                                    "mm" if kb == 11 else None)
                            pv_g4[gsl] = pv_last
                    b.wait("scalar", pv_last)
                    obs = h % 2
                    if evv is not None:
                        b.wait("scalar", evv)
                    b.wait("scalar", ob_busy5[obs])
                    evv = b.op("scalar", lambda s, obs=obs: s.copy(
                        ob[obs][:], ps[5][:]), "ev3")
                    b.wait("gpsimd", evv)
                    stv = b.dma("gpsimd", attn1r[:, h, 512 * qb:512 * (qb + 1)],
                                ob[obs][:], "st")
                    ob_busy5[obs] = stv
                    b.wait("tensor", evv)
                    plane_free = ("mm", b.cnt["mm"])
            return ("st", b.cnt["st"])

        p5_st = emit_attn1()
        if upto <= 6:
            return _finalize(nc, b, e, dump_aps, out_final, fbv)

        oir = sc["oi"].rearrange("(k p) t -> p k t", p=128)
        ST32 = act[:, 0:16384].bitcast(F32)
        w0T32, w2T32 = ST32[:, 0:2048], ST32[:, 2048:4096]
        w132, w1T32 = ST32[:, 4096:6144], ST32[:, 6144:8192]
        w0Tb, w2Tb = act[:, 16384:18432], act[:, 18432:20480]
        w1b, w1Tb = act[:, 20480:22528], act[:, 22528:24576]
        fkf, fvf, fqf = (act[:, 24576:28672], act[:, 28672:32768],
                         act[:, 32768:36864])
        fkt, fvt, fvl = (act[:, 36864:40960], act[:, 40960:45056],
                         act[:, 45056:49152])
        Gb, HPREb, DHb, SGb = (act[:, 49152:53248], act[:, 53248:57344],
                               act[:, 57344:61440], act[:, 61440:65536])
        sigb, hidb = wb2[0][:, 0:4096], wb2[0][:, 4096:8192]
        dgl0b, dhl2b = wb2[1][:, 0:4096], wb2[1][:, 4096:8192]
        u1A, u1B, u1C = u1[:, 0:4096], u1[:, 4096:8192], u1[:, 8192:12288]
        w0Ti = ap["w0T"].rearrange("h (dt p) c -> h p dt c", p=128)
        w2Ti = ap["w2T"].rearrange("h (dt p) c -> h p dt c", p=128)
        w1i = ap["w1n"].rearrange("h (dt p) c -> h p dt c", p=128)
        w1Ti = ap["w1T"].rearrange("h (dt p) c -> h p dt c", p=128)
        ttnc = e(nc.sbuf_tensor("ttnc", [128, 4], F32))
        lttn = b.dma("sync", ttnc[:], ap["ttnormcol"], "lx")

        def emit_ttt():
            b.wait("sync", p5_st)
            b.wait("vector", lttn)
            st_last = None
            for ht in range(4):
                b.wait("sync", st_last)
                li = b.dma("sync", w0T32.rearrange("p (dt c) -> p dt c", c=512),
                           w0Ti[ht], "lx")
                li = b.dma("sync", w2T32.rearrange("p (dt c) -> p dt c", c=512),
                           w2Ti[ht], "lx")
                li = b.dma("sync", w132.rearrange("p (dt c) -> p dt c", c=512),
                           w1i[ht], "lx")
                li = b.dma("sync", w1T32.rearrange("p (dt c) -> p dt c", c=512),
                           w1Ti[ht], "lx")
                b.wait("vector", li)
                if st_last is not None:
                    b.wait("vector", st_last)
                cm = None
                for (src32, dstb) in [(w0T32, w0Tb), (w2T32, w2Tb),
                                      (w132, w1b), (w1T32, w1Tb)]:
                    cm = b.op("vector", lambda v, s32=src32, db=dstb:
                              v.tensor_scalar(db, s32, 1.0, 0.0, ALU.mult,
                                              ALU.add), "ev2")
                cast_done = cm
                for c in range(4):
                    b.wait("sync", st_last)
                    lk = b.dma("sync", fkf.rearrange("p (dt t) -> p dt t", t=1024),
                               fkr[:, 4 * ht:4 * ht + 4,
                                   1024 * c:1024 * (c + 1)], "lx")
                    lv = b.dma("sync", fvf.rearrange("p (dt t) -> p dt t", t=1024),
                               fvr[:, 4 * ht:4 * ht + 4,
                                   1024 * c:1024 * (c + 1)], "lx")
                    lq = b.dma("sync", fqf.rearrange("p (dt t) -> p dt t", t=1024),
                               fqr[:, 4 * ht:4 * ht + 4,
                                   1024 * c:1024 * (c + 1)], "lx")
                    llr = b.dma("sync", fb2[:, 0:96].rearrange(
                        "p (tt c2) -> p tt c2", c2=12),
                        lrr[:, 8 * c:8 * (c + 1), 0:12], "lx")
                    b.wait("tensor", lk)
                    b.wait("tensor", lv)
                    for (srcf, dstt) in [(fkf, fkt), (fvf, fvt)]:
                        for ct in range(8):
                            trl = None
                            for dt in range(4):
                                trl = b.op("tensor", lambda t, srcf=srcf, ct=ct,
                                           dt=dt:
                                           t.transpose(
                                               psb[:, 128 * dt:128 * (dt + 1)],
                                               srcf[:, 1024 * dt + 128 * ct:
                                                    1024 * dt + 128 * (ct + 1)],
                                               cident[:]),
                                           "mmb" if dt == 3 else None)
                            b.wait("scalar", trl)
                            pt = b.op("scalar", lambda s, dstt=dstt, ct=ct:
                                      s.copy(dstt[:, 512 * ct:512 * (ct + 1)],
                                             psb[:, 0:512]), "ev3")
                            b.wait("tensor", pt)
                    tdone = pt
                    b.wait("vector", llr)
                    b.wait("vector", tdone)
                    fvll = None
                    for ct in range(8):
                        fvll = b.op("vector", lambda v, ct=ct, ht=ht: v.tensor_scalar(
                            fvl[:, 512 * ct:512 * (ct + 1)],
                            fvt[:, 512 * ct:512 * (ct + 1)],
                            fb2[:, 12 * ct + 4 + ht:12 * ct + 5 + ht], 0.0,
                            ALU.mult, ALU.add), "ev2" if ct == 7 else None)
                    b.wait("tensor", cast_done)
                    gm = {}
                    for (nm, rhsb, outb) in [("G", w0Tb, Gb), ("H", w2Tb, HPREb),
                                             ("D", w1b, DHb)]:
                        lhsf = fvf if nm == "D" else fkf
                        for ct in range(8):
                            bank = ps[ct % 2]
                            mml = None
                            for dt in range(4):
                                mml = b.op("tensor", lambda t, lhsf=lhsf, ct=ct,
                                           dt=dt, rhsb=rhsb, bank=bank:
                                           t.matmul(
                                               bank[:],
                                               lhsf[:, 1024 * dt + 128 * ct:
                                                    1024 * dt + 128 * (ct + 1)],
                                               rhsb[:, 512 * dt:512 * (dt + 1)],
                                               start=(dt == 0), stop=(dt == 3)),
                                           "mm" if dt == 3 else None)
                            b.wait("scalar", mml)
                            ev = b.op("scalar", lambda s, outb=outb, ct=ct,
                                      bank=bank:
                                      s.copy(outb[:, 512 * ct:512 * (ct + 1)],
                                             bank[:]), "ev3")
                            b.wait("tensor", ev)
                        gm[nm] = ev
                    b.wait("scalar", gm["G"])
                    sgm = b.op("scalar", lambda s: s.activation(
                        SGb, Gb, AF.Silu), "aux2")
                    sgm2 = b.op("scalar", lambda s: s.activation(
                        sigb, Gb, AF.Sigmoid), "aux2")
                    b.wait("vector", sgm2)
                    b.wait("vector", gm["H"])
                    hdm = b.op("vector", lambda v: v.tensor_tensor(
                        hidb, SGb, HPREb, ALU.mult), "ev2")
                    am = b.op("vector", lambda v: v.tensor_scalar(
                        u1A, sigb, -1.0, 1.0, ALU.mult, ALU.add), "aux3")
                    b.wait("vector", am)
                    t1m = b.op("vector", lambda v: v.tensor_tensor(
                        u1B, Gb, u1A, ALU.mult), "aux3")
                    b.wait("vector", t1m)
                    t2m = b.op("vector", lambda v: v.tensor_scalar(
                        u1A, u1B, 1.0, 1.0, ALU.mult, ALU.add), "aux3")
                    b.wait("vector", t2m)
                    ttm = b.op("vector", lambda v: v.tensor_tensor(
                        u1B, sigb, u1A, ALU.mult), "aux3")
                    b.wait("vector", gm["D"])
                    d1m = b.op("vector", lambda v: v.tensor_tensor(
                        u1C, DHb, HPREb, ALU.mult), "aux3")
                    b.wait("vector", d1m)
                    b.wait("vector", ttm)
                    dgm = b.op("vector", lambda v: v.tensor_tensor(
                        u1A, u1C, u1B, ALU.mult), "aux3")
                    b.wait("vector", dgm)
                    dgl = None
                    for ct in range(8):
                        dgl = b.op("vector", lambda v, ct=ct, ht=ht: v.tensor_scalar(
                            dgl0b[:, 512 * ct:512 * (ct + 1)],
                            u1A[:, 512 * ct:512 * (ct + 1)],
                            fb2[:, 12 * ct + ht:12 * ct + 1 + ht], 0.0,
                            ALU.mult, ALU.add), "ev2" if ct == 7 else None)
                    dhm = b.op("vector", lambda v: v.tensor_tensor(
                        u1B, DHb, SGb, ALU.mult), "aux3")
                    b.wait("vector", dhm)
                    dhl = None
                    for ct in range(8):
                        dhl = b.op("vector", lambda v, ct=ct, ht=ht: v.tensor_scalar(
                            dhl2b[:, 512 * ct:512 * (ct + 1)],
                            u1B[:, 512 * ct:512 * (ct + 1)],
                            fb2[:, 12 * ct + 8 + ht:12 * ct + 9 + ht], 0.0,
                            ALU.mult, ALU.add), "ev2" if ct == 7 else None)
                    b.wait("tensor", dgl)
                    b.wait("tensor", dhl)
                    b.wait("tensor", hdm)
                    b.wait("tensor", fvll)
                    caml = None
                    for (st32, stb, lhsb, rhsb2) in [
                            (w0T32, w0Tb, fkt, dgl0b), (w2T32, w2Tb, fkt, dhl2b),
                            (w132, w1b, fvl, hidb), (w1T32, w1Tb, hidb, fvl)]:
                        for m in range(4):
                            bank = ps[2 + (m % 2)]
                            mml = None
                            for ct in range(8):
                                mml = b.op("tensor", lambda t, lhsb=lhsb, ct=ct,
                                           m=m, rhsb2=rhsb2, bank=bank:
                                           t.matmul(
                                               bank[:],
                                               lhsb[:, 512 * ct + 128 * m:
                                                    512 * ct + 128 * (m + 1)],
                                               rhsb2[:, 512 * ct:512 * (ct + 1)],
                                               start=(ct == 0), stop=(ct == 7)),
                                           "mm" if ct == 7 else None)
                            b.wait("vector", mml)
                            adm = b.op("vector", lambda v, st32=st32, m=m,
                                       bank=bank:
                                       v.tensor_tensor(
                                           st32[:, 512 * m:512 * (m + 1)],
                                           st32[:, 512 * m:512 * (m + 1)],
                                           bank[:], ALU.add), "aux3")
                            b.wait("vector", adm)
                            caml = b.op("vector", lambda v, st32=st32, stb=stb,
                                        m=m:
                                        v.tensor_scalar(
                                            stb[:, 512 * m:512 * (m + 1)],
                                            st32[:, 512 * m:512 * (m + 1)],
                                            1.0, 0.0, ALU.mult, ALU.add), "ev2")
                            b.wait("tensor", caml)
                    cast_done = caml
                    b.wait("tensor", lq)
                    for (rhsb3, outb) in [(w0Tb, Gb), (w2Tb, HPREb)]:
                        for hh in range(4):
                            for cc in range(2):
                                bank = ps[4]
                                mml = None
                                for dt in range(4):
                                    mml = b.op("tensor", lambda t, hh=hh, cc=cc,
                                               dt=dt, rhsb3=rhsb3, bank=bank:
                                               t.matmul(
                                                   bank[:],
                                                   rhsb3[:, 512 * dt + 128 * hh:
                                                         512 * dt + 128 * (hh + 1)],
                                                   fqf[:, 1024 * dt + 512 * cc:
                                                       1024 * dt + 512 * (cc + 1)],
                                                   start=(dt == 0), stop=(dt == 3)),
                                               "mm" if dt == 3 else None)
                                b.wait("scalar", mml)
                                ev = b.op("scalar", lambda s, outb=outb, hh=hh,
                                          cc=cc, bank=bank:
                                          s.copy(outb[:, 1024 * hh + 512 * cc:
                                                      1024 * hh + 512 * (cc + 1)],
                                                 bank[:]), "ev3")
                                b.wait("tensor", ev)
                    sgq = b.op("scalar", lambda s: s.activation(
                        SGb, Gb, AF.Silu), "aux2")
                    b.wait("vector", sgq)
                    hqm = b.op("vector", lambda v: v.tensor_tensor(
                        DHb, SGb, HPREb, ALU.mult), "ev2")
                    b.wait("tensor", hqm)
                    for ct in range(8):
                        bank = ps[5]
                        mml = None
                        for hh in range(4):
                            mml = b.op("tensor", lambda t, hh=hh, ct=ct, bank=bank:
                                       t.matmul(
                                           bank[:],
                                           DHb[:, 1024 * hh + 128 * ct:
                                               1024 * hh + 128 * (ct + 1)],
                                           w1Tb[:, 512 * hh:512 * (hh + 1)],
                                           start=(hh == 0), stop=(hh == 3)),
                                       "mm" if hh == 3 else None)
                        b.wait("scalar", mml)
                        sq6 = b.op("scalar", lambda s, bank=bank: s.activation(
                            u1B[:, 0:512], bank[:], AF.Square), "aux2")
                        b.wait("vector", sq6)
                        rd6 = b.op("vector", lambda v: v.tensor_reduce(
                            colv[:, 140:141], u1B[:, 0:512], AX.X, ALU.add),
                            "aux3")
                        b.wait("vector", rd6)
                        vm6 = b.op("vector", lambda v: v.tensor_scalar(
                            colv[:, 141:142], colv[:, 140:141], 1.0 / 512.0,
                            1e-6, ALU.mult, ALU.add), "aux3")
                        b.wait("scalar", vm6)
                        ln6 = b.op("scalar", lambda s: s.activation(
                            colv[:, 142:143], colv[:, 141:142], AF.Ln), "aux2")
                        b.wait("scalar", ln6)
                        rs6 = b.op("scalar", lambda s: s.activation(
                            colv[:, 143:144], colv[:, 142:143], AF.Exp,
                            scale=-0.5), "rdy")
                        b.wait("vector", rs6)
                        oin = b.op("vector", lambda v, bank=bank: v.tensor_scalar(
                            u1A[:, 0:512], bank[:], colv[:, 143:144], 0.0,
                            ALU.mult, ALU.add), "aux3")
                        b.wait("tensor", oin)
                        for j in range(4):
                            trj = b.op("tensor", lambda t, j=j: t.transpose(
                                psb[:, 128 * j:128 * (j + 1)],
                                u1A[:, 128 * j:128 * (j + 1)], cident[:]),
                                "mmb" if j == 3 else None)
                        b.wait("scalar", trj)
                        stj = None
                        for j in range(4):
                            obs = j % 2
                            b.wait("scalar", ob_busy6[obs])
                            evj = b.op("scalar", lambda s, j=j, obs=obs:
                                       s.activation(
                                           ob[obs][:, 0:128],
                                           psb[:, 128 * j:128 * (j + 1)],
                                           AF.Copy, scale=ttnc[:, j:j + 1]),
                                       "ev3")
                            b.wait("gpsimd", evj)
                            stj = b.dma("gpsimd",
                                        oir[:, 4 * ht + j,
                                            1024 * c + 128 * ct:
                                            1024 * c + 128 * (ct + 1)],
                                        ob[obs][:, 0:128], "st")
                            ob_busy6[obs] = stj
                        b.wait("tensor", stj)
                        st_last = stj
            return ("st", b.cnt["st"])

        ob_busy6 = [None, None]
        p6_st = emit_ttt()
        if upto <= 7:
            return _finalize(nc, b, e, dump_aps, out_final, fbv)

        xn1bor = sc["xn1bo"].rearrange("(k p) t -> p k t", p=128)
        xn1for = sc["xn1fo"].rearrange("(k p) t -> p k t", p=128)
        x1or = sc["x1o"].rearrange("(k p) t -> p k t", p=128)
        oior = sc["oio"].rearrange("(k p) t -> p k t", p=128)
        attn1r2 = sc["attn1"].rearrange("(k p) t -> p k t", p=128)
        ylatr = sc["ylat"].rearrange("(k p) t -> p k t", p=128)
        hshr = sc["hsh"].rearrange("(k p) t -> p k t", p=128)
        fc1r = ap["fc1T"].rearrange("(k p) c -> p k c", p=128)
        gwr = ap["gatewT"].rearrange("(k p) c -> p k c", p=128)
        w1er = ap["moew1T"].rearrange("h (k p) c -> h p k c", p=128)
        w2er = ap["moew2T"].rearrange("h (k p) c -> h p k c", p=128)
        fc2r = ap["fc2T"].rearrange("(k p) c -> p k c", p=128)
        shupr = ap["shupT"].rearrange("(k p) c -> p k c", p=128)
        shdnr = ap["shdnT"].rearrange("(k p) c -> p k c", p=128)
        wor = ap["woT"].rearrange("(k p) c -> p k c", p=128)
        outr = out_final.rearrange("(k p) t -> p k t", p=128)
        crow2 = e(nc.sbuf_tensor("crow2", [1, 256], F32))

        xn1o_sb = act[:, 0:32768]
        xlat_sb = act[:, 32768:49152]
        w2e_sb = [act[:, 49152:57344], act[:, 57344:65536]]
        cwt = u1[:, 0:8192].bitcast(F32)[0:32, 0:2048]
        hsh_sb = u1[:, 0:8192]
        ylat_sb = u1[:, 8192:12288].bitcast(F32)
        asum_sb = u1[:, 0:8192]
        ylb_sb = u1[:, 8192:12288]
        xn1f_sb = wb2[0][:, 0:4096].bitcast(F32)

        def emit_p7():
            def setup(E):
                r = E.alloc_register("roff7")
                E.reg_load(r, ccfg[0:1, 0:1])
                p7v["off"] = E.snap(r, min_val=0, max_val=2048)
            b.add("sync", setup)
            b.wait("sync", p6_st)
            for (dst, srcv) in [(xn1bor, xn1br), (xn1for, xn1fr),
                                (x1or, x1rr), (oior, oir)]:
                b.add("sync", lambda E, dst=dst, srcv=srcv: E.dma_start(
                    out=dst, in_=srcv[:, :, bass.ds(p7v["off"], 2048)]
                ).then_inc(b.sem["st2"], 16))
                b.cnt["st2"] += 16
            stg = ("st2", b.cnt["st2"])
            b.wait("sync", stg)
            lx7 = None
            for j in range(4):
                lx7 = b.dma("sync",
                            xn1o_sb.rearrange("p (k t) -> p k t", t=2048)
                            [:, :, 512 * j:512 * (j + 1)],
                            xn1bor[:, :, 512 * j:512 * (j + 1)], "lx")
            lgw = b.dma("sync", fb2[:].rearrange("p (k c) -> p k c", c=32),
                        gwr, "lx")
            lgb = b.dma("sync", colv[:, 96:128], ap["gbias"], "lx")
            b.wait("tensor", lx7)
            wfree = None
            for lm in range(8):
                b.wait("gpsimd", wfree)
                lw7 = b.dma("gpsimd", wb[1][:].rearrange("p (k c) -> p k c", c=128),
                            fc1r[:, :, 128 * lm:128 * (lm + 1)], "lw")
                b.wait("tensor", lw7)
                mml = None
                for tb4 in range(4):
                    bank = ps[tb4 % 2]
                    if lm + tb4 > 0:
                        b.wait("tensor", xlev[-2] if len(xlev) >= 2 else None)
                    for k in range(NDT):
                        mml = b.op("tensor", lambda t, k=k, tb4=tb4, bank=bank:
                                   t.matmul(bank[:],
                                            wb[1][:, 128 * k:128 * (k + 1)],
                                            xn1o_sb[:, 2048 * k + 512 * tb4:
                                                    2048 * k + 512 * (tb4 + 1)],
                                            start=(k == 0), stop=(k == NDT - 1)),
                                   "mm" if k == NDT - 1 else None)
                    b.wait("scalar", mml)
                    xle = b.op("scalar", lambda s, lm=lm, tb4=tb4, bank=bank:
                               s.copy(xlat_sb[:, 2048 * lm + 512 * tb4:
                                              2048 * lm + 512 * (tb4 + 1)],
                                      bank[:]), "ev3")
                    xlev.append(xle)
                wfree = mml
            xlat_done = xlev[-1]
            b.wait("tensor", lgw)
            b.wait("vector", lgb)
            RC = 0
            selbuf = fbv[0]
            t1buf = fbv[1]
            cwev = None
            lgmm = None
            for tt in range(16):
                b.wait("sync", lgmm)
                lxf = b.dma("sync", xn1f_sb[:, 0:2048].rearrange(
                    "p (k c) -> p k c", c=128),
                    xn1for[:, :, 128 * tt:128 * (tt + 1)], "lx")
                if tt == 0:
                    b.wait("gpsimd", lxf)
                    dxk = b.dma("gpsimd", sc["dbg"][:, 0:2048],
                                xn1f_sb[:, 0:2048], "st2")
                b.wait("tensor", lxf)
                if cwev is not None:
                    b.wait("tensor", cwev)
                mml = None
                for k in range(NDT):
                    mml = b.op("tensor", lambda t, k=k: t.matmul(
                        ps[6][:, 0:32],
                        xn1f_sb[:, 128 * k:128 * (k + 1)],
                        fb2[:, 32 * k:32 * (k + 1)],
                        start=(k == 0), stop=(k == NDT - 1)),
                        "mm" if k == NDT - 1 else None)
                lgmm = mml
                b.wait("scalar", mml)
                sgm7 = b.op("scalar", lambda s: s.activation(
                    selbuf[:, 0:32], ps[6][:, 0:32], AF.Sigmoid), "aux2")
                b.wait("vector", sgm7)
                sbm = b.op("vector", lambda v: v.tensor_tensor(
                    selbuf[:, 32:64], selbuf[:, 0:32], colv[:, 96:128],
                    ALU.add), "aux3")
                b.wait("vector", sbm)
                gl = None
                for g in range(4):
                    m1 = b.op("vector", lambda v, g=g: v.tensor_reduce(
                        colv[:, RC + 0:RC + 1], selbuf[:, 32 + 8 * g:40 + 8 * g],
                        AX.X, ALU.max), "aux3")
                    b.wait("vector", m1)
                    e1 = b.op("vector", lambda v, g=g: v.tensor_scalar(
                        t1buf[:, 0:8], selbuf[:, 32 + 8 * g:40 + 8 * g],
                        colv[:, RC + 0:RC + 1], 0.0, ALU.is_equal, ALU.add),
                        "aux3")
                    b.wait("vector", e1)
                    ms = b.op("vector", lambda v, g=g: v.scalar_tensor_tensor(
                        t1buf[:, 8:16], t1buf[:, 0:8], -BIG,
                        selbuf[:, 32 + 8 * g:40 + 8 * g], ALU.mult, ALU.add),
                        "aux3")
                    b.wait("vector", ms)
                    m2 = b.op("vector", lambda v, g=g: v.tensor_reduce(
                        colv[:, RC + 1:RC + 2], t1buf[:, 8:16], AX.X, ALU.max),
                        "aux3")
                    b.wait("vector", m2)
                    gl = b.op("vector", lambda v, g=g: v.tensor_tensor(
                        colv[:, RC + 2 + g:RC + 3 + g], colv[:, RC + 0:RC + 1],
                        colv[:, RC + 1:RC + 2], ALU.add), "aux3")
                    b.wait("vector", gl)
                gm1 = b.op("vector", lambda v: v.tensor_reduce(
                    colv[:, RC + 6:RC + 7], colv[:, RC + 2:RC + 6], AX.X,
                    ALU.max), "aux3")
                b.wait("vector", gm1)
                eg = b.op("vector", lambda v: v.tensor_scalar(
                    t1buf[:, 0:4], colv[:, RC + 2:RC + 6],
                    colv[:, RC + 6:RC + 7], 0.0, ALU.is_equal, ALU.add), "aux3")
                b.wait("vector", eg)
                msg_ = b.op("vector", lambda v: v.scalar_tensor_tensor(
                    t1buf[:, 4:8], t1buf[:, 0:4], -BIG, colv[:, RC + 2:RC + 6],
                    ALU.mult, ALU.add), "aux3")
                b.wait("vector", msg_)
                gm2 = b.op("vector", lambda v: v.tensor_reduce(
                    colv[:, RC + 7:RC + 8], t1buf[:, 4:8], AX.X, ALU.max),
                    "aux3")
                b.wait("vector", gm2)
                gmask = b.op("vector", lambda v: v.tensor_scalar(
                    t1buf[:, 0:4], colv[:, RC + 2:RC + 6],
                    colv[:, RC + 7:RC + 8], 0.0, ALU.is_ge, ALU.add), "aux3")
                b.wait("vector", gmask)
                gx = None
                for g in range(4):
                    gx = b.op("vector", lambda v, g=g: v.tensor_scalar(
                        t1buf[:, 8 + 8 * g:16 + 8 * g],
                        selbuf[:, 32 + 8 * g:40 + 8 * g], 0.0,
                        t1buf[:, g:g + 1], ALU.mult, ALU.add), "aux3")
                b.wait("vector", gx)
                t1m7 = b.op("vector", lambda v: v.tensor_scalar(
                    selbuf[:, 64:96], t1buf[:, 8:40], BIG, -BIG, ALU.mult,
                    ALU.add), "aux3")
                b.wait("vector", t1m7)
                msk0 = b.op("vector", lambda v: v.tensor_tensor(
                    t1buf[:, 40:72], selbuf[:, 32:64], t1buf[:, 8:40],
                    ALU.mult), "aux3")
                b.wait("vector", msk0)
                msk1 = b.op("vector", lambda v: v.tensor_tensor(
                    selbuf[:, 64:96], t1buf[:, 40:72], selbuf[:, 64:96],
                    ALU.add), "aux3")
                b.wait("vector", msk1)
                zs = b.op("vector", lambda v: v.memset(t1buf[:, 72:104], 0.0),
                          "aux3")
                b.wait("vector", zs)
                cur = 64
                oth = 96
                for it8 in range(8):
                    mi = b.op("vector", lambda v, cur=cur: v.tensor_reduce(
                        colv[:, RC + 8:RC + 9], selbuf[:, cur:cur + 32], AX.X,
                        ALU.max), "aux3")
                    b.wait("vector", mi)
                    ei = b.op("vector", lambda v, cur=cur: v.tensor_scalar(
                        t1buf[:, 104:136], selbuf[:, cur:cur + 32],
                        colv[:, RC + 8:RC + 9], 0.0, ALU.is_equal, ALU.add),
                        "aux3")
                    b.wait("vector", ei)
                    se = b.op("vector", lambda v: v.tensor_tensor(
                        t1buf[:, 72:104], t1buf[:, 72:104], t1buf[:, 104:136],
                        ALU.add), "aux3")
                    b.wait("vector", se)
                    mu = b.op("vector", lambda v, cur=cur, oth=oth:
                              v.scalar_tensor_tensor(
                                  selbuf[:, oth:oth + 32], t1buf[:, 104:136],
                                  -BIG, selbuf[:, cur:cur + 32], ALU.mult,
                                  ALU.add), "aux3")
                    b.wait("vector", mu)
                    cur, oth = oth, cur
                cwm = b.op("vector", lambda v: v.tensor_tensor(
                    t1buf[:, 104:136], selbuf[:, 0:32], t1buf[:, 72:104],
                    ALU.mult), "aux3")
                b.wait("vector", cwm)
                if tt == 0:
                    b.wait("gpsimd", cwm)
                    b.dma("gpsimd", sc["dbg"][:, 2048:2080], selbuf[:, 0:32],
                          "st2")
                    b.dma("gpsimd", sc["dbg"][:, 2080:2112], selbuf[:, 32:64],
                          "st2")
                    b.dma("gpsimd", sc["dbg"][:, 2112:2144], t1buf[:, 72:104],
                          "st2")
                    b.dma("gpsimd", sc["dbg"][:, 2144:2176], t1buf[:, 104:136],
                          "st2")
                    dmk = b.dma("gpsimd", sc["dbg"][:, 2176:2188],
                                colv[:, RC:RC + 12], "st2")
                    b.wait("vector", dmk)
                    b.wait("scalar", dmk)
                rsm = b.op("vector", lambda v: v.tensor_reduce(
                    colv[:, RC + 8:RC + 9], t1buf[:, 104:136], AX.X, ALU.add),
                    "aux3")
                b.wait("vector", rsm)
                rcm = b.op("vector", lambda v: v.reciprocal(
                    colv[:, RC + 9:RC + 10], colv[:, RC + 8:RC + 9]), "aux3")
                b.wait("vector", rcm)
                r25 = b.op("vector", lambda v: v.tensor_scalar(
                    colv[:, RC + 10:RC + 11], colv[:, RC + 9:RC + 10], 2.5, 0.0,
                    ALU.mult, ALU.add), "aux3")
                b.wait("vector", r25)
                cwf = b.op("vector", lambda v: v.tensor_scalar(
                    t1buf[:, 72:104], t1buf[:, 104:136],
                    colv[:, RC + 10:RC + 11], 0.0, ALU.mult, ALU.add), "aux3")
                b.wait("tensor", cwf)
                b.wait("tensor", lcidf)
                trc = b.op("tensor", lambda t: t.transpose(
                    ps[5][0:32, 0:128], t1buf[:, 72:104], cidf[:]), "mmb")
                b.wait("scalar", trc)
                cwev = b.op("scalar", lambda s, tt=tt: s.copy(
                    cwt[:, 128 * tt:128 * (tt + 1)], ps[5][0:32, 0:128]),
                    "ev3")
            cw_done = cwev
            zv7 = b.op("vector", lambda v: v.memset(fbv[0][:], 0.0), "aux")
            b.wait("gpsimd", zv7)
            zst = None
            for k8 in range(8):
                for j4 in range(4):
                    zst = b.dma("gpsimd", ylatr[:, k8, 512 * j4:512 * (j4 + 1)],
                                fbv[0][:].bitcast(F32) if False else fbv[0][:],
                                "st")
            b.wait("gpsimd", zst)
            yslot = {}
            wfree2 = [cw_done, cw_done]
            b.wait("scalar", cw_done)
            ymarks = []
            for ex in range(32):
                sl = ex % 2
                b.wait("gpsimd", wfree2[sl])
                b.dma("gpsimd", wb2[sl][:].rearrange("p (k c) -> p k c", c=1024),
                      w1er[ex], "lw")
                lwe = b.dma("gpsimd",
                            w2e_sb[sl].rearrange("p (k c) -> p k c", c=1024),
                            w2er[ex], "lw")
                b.wait("tensor", lwe)
                if ex == 0:
                    b.wait("tensor", xlat_done)
                mml = None
                for tb8 in range(8):
                    b.wait("sync", bc7m[0])
                    lcw = b.dma("sync", crow2[:],
                                cwt[ex:ex + 1, 256 * tb8:256 * (tb8 + 1)], "lx")
                    b.wait("tensor", lcw)
                    b.wait("tensor", mb7m[0])
                    bc7 = b.op("tensor", lambda t: t.matmul(
                        ps[6][:, 0:256], conesr[:], crow2[:],
                        start=True, stop=True), "mmb")
                    bc7m[0] = bc7
                    b.wait("scalar", bc7)
                    mb7 = b.op("scalar", lambda s: s.copy(
                        mrow[:, 0:256], ps[6][:, 0:256]), "rdy")
                    mb7m[0] = mb7
                    hev = None
                    hloc = []
                    for im in range(8):
                        bank = ps[im % 2]
                        if im >= 2:
                            b.wait("tensor", hloc[im - 2])
                        for lt in range(8):
                            mml = b.op("tensor", lambda t, sl=sl, lt=lt, im=im,
                                       tb8=tb8, bank=bank:
                                       t.matmul(
                                           bank[:, 0:256],
                                           wb2[sl][:, 1024 * lt + 128 * im:
                                                   1024 * lt + 128 * (im + 1)],
                                           xlat_sb[:, 2048 * lt + 256 * tb8:
                                                   2048 * lt + 256 * (tb8 + 1)],
                                           start=(lt == 0), stop=(lt == 7)),
                                       "mm" if lt == 7 else None)
                        b.wait("scalar", mml)
                        b.wait("scalar", hloc[-1] if hloc else None)
                        rl7 = b.op("scalar", lambda s, bank=bank: s.activation(
                            ob[2][:, 0:256], bank[:, 0:256], AF.Relu), "aux2")
                        b.wait("vector", rl7)
                        b.wait("vector", mb7)
                        m17 = b.op("vector", lambda v: v.tensor_tensor(
                            ob[3][:, 0:256], ob[2][:, 0:256], mrow[:, 0:256],
                            ALU.mult), "aux3")
                        b.wait("vector", m17)
                        hp = b.op("vector", lambda v, im=im: v.tensor_tensor(
                            wb[1][:, 256 * im:256 * (im + 1)], ob[2][:, 0:256],
                            ob[3][:, 0:256], ALU.mult), "ev2")
                        hloc.append(hp)
                        hev = hp
                    if ex == 0 and tb8 == 0:
                        b.wait("gpsimd", hev)
                        b.wait("gpsimd", mb7)
                        b.dma("gpsimd", sc["dbg"][:, 2188:2252].bitcast(BF16)
                              if False else sc["dbg"][:, 2188:2252],
                              mrow[:, 0:64], "st2")
                        dmk2 = b.dma("gpsimd", sc["dbg"][:, 2252:3276],
                                     wb[1][:].bitcast(F32), "st2")
                        b.wait("scalar", dmk2)
                        b.wait("tensor", dmk2)
                    b.wait("tensor", hev)
                    for lm in range(8):
                        bank = ps[2 + lm % 2]
                        if len(ymarks) >= 2:
                            b.wait("tensor", ymarks[-2])
                        yml = None
                        for it in range(8):
                            yml = b.op("tensor", lambda t, sl=sl, it=it, lm=lm,
                                       bank=bank:
                                       t.matmul(
                                           bank[:, 0:256],
                                           w2e_sb[sl][:, 1024 * it + 128 * lm:
                                                      1024 * it + 128 * (lm + 1)],
                                           wb[1][:, 256 * it:256 * (it + 1)],
                                           start=(it == 0), stop=(it == 7)),
                                       "mm" if it == 7 else None)
                        b.wait("scalar", yml)
                        fsl = lm % 2
                        if len(ymarks) >= 2:
                            b.wait("scalar", ymarks[-2])
                        ye = b.op("scalar", lambda s, bank=bank, fsl=fsl: s.copy(
                            fbv[fsl][:, 0:256], bank[:, 0:256]), "ev3")
                        b.wait("gpsimd", ye)
                        b.wait("gpsimd", yslot.get((lm, tb8)))
                        def yacc(E, lm=lm, tb8=tb8, fsl=fsl):
                            E.dma_start(out=ylatr[:, lm, 256 * tb8:256 * (tb8 + 1)],
                                        in_=fbv[fsl][:, 0:256],
                                        accum_op=ALU.add).then_inc(
                                            b.sem["st"], 16)
                        b.add("gpsimd", yacc)
                        b.cnt["st"] += 16
                        ymarks.append(("st", b.cnt["st"]))
                        yslot[(lm, tb8)] = ymarks[-1]
                wfree2[sl] = mml
            y_done = ("st", b.cnt["st"])
            wfree3 = ("mm", b.cnt["mm"])
            shst = None
            for ism in range(32):
                b.wait("gpsimd", wfree3)
                lws = b.dma("gpsimd", wb[1][:].rearrange("p (k c) -> p k c",
                                                         c=128),
                            shupr[:, :, 128 * ism:128 * (ism + 1)], "lw")
                b.wait("tensor", lws)
                mml = None
                for tb4 in range(4):
                    bank = ps[4 + tb4 % 2]
                    if shev:
                        b.wait("tensor", shev[-2] if len(shev) >= 2 else None)
                    for k in range(NDT):
                        mml = b.op("tensor", lambda t, k=k, tb4=tb4, bank=bank:
                                   t.matmul(bank[:],
                                            wb[1][:, 128 * k:128 * (k + 1)],
                                            xn1o_sb[:, 2048 * k + 512 * tb4:
                                                    2048 * k + 512 * (tb4 + 1)],
                                            start=(k == 0), stop=(k == NDT - 1)),
                                   "mm" if k == NDT - 1 else None)
                    b.wait("scalar", mml)
                    b.wait("scalar", shev[-1] if shev else None)
                    rl8 = b.op("scalar", lambda s, bank=bank: s.activation(
                        ob[2][:], bank[:], AF.Relu), "aux2")
                    b.wait("vector", rl8)
                    sq8 = b.op("vector", lambda v: v.tensor_tensor(
                        ob[3][:], ob[2][:], ob[2][:], ALU.mult), "ev2")
                    shev.append(sq8)
                    b.wait("gpsimd", sq8)
                    shst = b.dma("gpsimd",
                                 hshr[:, ism, 512 * tb4:512 * (tb4 + 1)],
                                 ob[3][:], "st")
                wfree3 = mml
            sh_done = ("st", b.cnt["st"])
            wo_sb = act[:, 32768:49152]
            fc2_sb = act[:, 49152:53248]
            shdn_sb = act[:, 0:16384]
            ylf_sb = wb2[1][:, 0:8192].bitcast(F32)
            ylb2 = u1[:, 8192:12288]
            b.wait("gpsimd", sh_done)
            b.wait("gpsimd", y_done)
            b.wait("sync", sh_done)
            b.wait("sync", y_done)
            last_out = None
            for ocg in range(4):
                if last_out is not None:
                    b.wait("gpsimd", last_out)
                b.dma("gpsimd", wo_sb[:, 0:8192].rearrange(
                    "p (k c) -> p k c", c=512),
                    wor[:, :, 512 * ocg:512 * (ocg + 1)], "lw")
                b.dma("gpsimd", fc2_sb.rearrange("p (k c) -> p k c", c=512),
                      fc2r[:, :, 512 * ocg:512 * (ocg + 1)], "lw")
                lwo = b.dma("gpsimd", shdn_sb.rearrange(
                    "p (k c) -> p k c", c=512),
                    shdnr[:, :, 512 * ocg:512 * (ocg + 1)], "lw")
                b.wait("tensor", lwo)
                for tb4 in range(4):
                    b.wait("sync", rhs_free[0])
                    la7 = b.dma("sync", asum_sb.rearrange(
                        "p (k t) -> p k t", t=512),
                        attn1r2[:, :, 512 * tb4:512 * (tb4 + 1)], "lx")
                    lo7 = b.dma("sync", wb2[0][:].rearrange(
                        "p (k t) -> p k t", t=512),
                        oior[:, :, 512 * tb4:512 * (tb4 + 1)], "lx")
                    ly7 = b.dma("sync", ylf_sb.rearrange(
                        "p (k t) -> p k t", t=512),
                        ylatr[:, :, 512 * tb4:512 * (tb4 + 1)], "lx")
                    b.wait("vector", la7)
                    b.wait("vector", lo7)
                    adm7 = b.op("vector", lambda v: v.tensor_tensor(
                        asum_sb, asum_sb, wb2[0][:], ALU.add), "ev2")
                    b.wait("sync", adm7)
                    lh0 = b.dma("sync", hsh0_sb.rearrange(
                        "p (k t) -> p k t", t=512),
                        hshr[:, 0:16, 512 * tb4:512 * (tb4 + 1)], "lx")
                    b.wait("vector", ly7)
                    ycv = b.op("vector", lambda v: v.tensor_scalar(
                        ylb2, ylf_sb, 1.0, 0.0, ALU.mult, ALU.add), "ev2")
                    b.wait("tensor", adm7)
                    b.wait("tensor", ycv)
                    b.wait("tensor", lh0)
                    mml = None
                    for oct_ in range(4):
                        bank = ps[oct_]
                        b.wait("tensor", oct_free[oct_])
                        for k in range(16):
                            b.op("tensor", lambda t, k=k, oct_=oct_, bank=bank:
                                 t.matmul(bank[:],
                                          wo_sb[:, 512 * k + 128 * oct_:
                                                512 * k + 128 * (oct_ + 1)],
                                          asum_sb[:, 512 * k:512 * (k + 1)],
                                          start=(k == 0), stop=False))
                        for k in range(8):
                            b.op("tensor", lambda t, k=k, oct_=oct_, bank=bank:
                                 t.matmul(bank[:],
                                          fc2_sb[:, 512 * k + 128 * oct_:
                                                 512 * k + 128 * (oct_ + 1)],
                                          ylb2[:, 512 * k:512 * (k + 1)],
                                          start=False, stop=False))
                        for k in range(16):
                            mml = b.op("tensor",
                                       lambda t, k=k, oct_=oct_, bank=bank:
                                       t.matmul(
                                           bank[:],
                                           shdn_sb[:, 512 * k + 128 * oct_:
                                                   512 * k + 128 * (oct_ + 1)],
                                           hsh0_sb[:, 512 * k:512 * (k + 1)],
                                           start=False, stop=False),
                                       "mm" if k == 15 else None)
                    b.wait("sync", mml)
                    lh1 = b.dma("sync", hsh0_sb.rearrange(
                        "p (k t) -> p k t", t=512),
                        hshr[:, 16:32, 512 * tb4:512 * (tb4 + 1)], "lx")
                    b.wait("tensor", lh1)
                    for oct_ in range(4):
                        bank = ps[oct_]
                        mml2 = None
                        for k in range(16):
                            mml2 = b.op("tensor",
                                        lambda t, k=k, oct_=oct_, bank=bank:
                                        t.matmul(
                                            bank[:],
                                            shdn_sb[:, 512 * (16 + k) +
                                                    128 * oct_:
                                                    512 * (16 + k) +
                                                    128 * (oct_ + 1)],
                                            hsh0_sb[:, 512 * k:512 * (k + 1)],
                                            start=False, stop=(k == 15)),
                                        "mm" if k == 15 else None)
                        b.wait("sync", x1_free[0])
                        lx1 = b.dma("sync", fbv[0][:],
                                    x1or[:, 4 * ocg + oct_,
                                         512 * tb4:512 * (tb4 + 1)], "lx")
                        b.wait("vector", mml2)
                        b.wait("vector", lx1)
                        b.wait("vector", out_free[oct_ % 2])
                        oadd = b.op("vector", lambda v, bank=bank, oct_=oct_:
                                    v.scalar_tensor_tensor(
                                        fbv2_out[oct_ % 2], bank[:], 0.0,
                                        fbv[0][:], ALU.bypass, ALU.add), "ev")
                        oct_free[oct_] = oadd
                        x1_free[0] = oadd
                        b.wait("gpsimd", oadd)
                        last_out = b.dma("gpsimd",
                                         outr[:, 4 * ocg + oct_,
                                              512 * tb4:512 * (tb4 + 1)],
                                         fbv2_out[oct_ % 2], "st")
                        out_free[oct_ % 2] = last_out
                    rhs_free[0] = ("mm", b.cnt["mm"])
            return last_out

        xlev = []
        hpr = []
        shev = []
        bc7m = [None]
        mb7m = [None]
        rhs_free = [None]
        x1_free = [None]
        oct_free = [None, None, None, None]
        out_free = [None, None]
        hsh0_sb = wb2[0]
        fbv2_out = [wb[0][:, 0:1024].bitcast(F32), wb[0][:, 1024:2048].bitcast(F32)]
        cidf = mrow[:, 256:384]
        p7v = {}
        lcidf = b.dma("sync", cidf[:], ap["dmask"] if False else ap["identf"],
                      "lx")
        p7_last = emit_p7()

        b.wait("gpsimd", p7_last)
        b.dma("gpsimd", sc["dbg"][0:32, 0:2048], cwt, "fin")

        return _finalize(nc, b, e, dump_aps, out_final, fbv, write_zero=False)


def _finalize(nc, b, e, dump_aps, out_final, fbv, write_zero=True):
    for name in ["st", "st2", "ev", "ev3", "mm"]:
        if b.cnt[name]:
            b.wait("gpsimd", (name, b.cnt[name]))
    for dd, dsrc in dump_aps:
        b.dma("gpsimd", dd, dsrc, "fin")
    if write_zero:
        zv = b.op("vector", lambda v: v.memset(fbv[0][:], 0.0), "aux3")
        b.wait("gpsimd", zv)
        for ct in range(NDT):
            for tt in range(4):
                b.dma("gpsimd",
                      out_final[128 * ct:128 * (ct + 1), 512 * tt:512 * (tt + 1)],
                      fbv[0][:], "fin")
    b.wait("gpsimd", ("fin", b.cnt["fin"]))

    block = e(nc.Block())

    @block.sync
    def _(E):
        for fn in b.ops["sync"]:
            fn(E)

    @block.gpsimd
    def _(E):
        for fn in b.ops["gpsimd"]:
            fn(E)

    @block.tensor
    def _(E):
        for fn in b.ops["tensor"]:
            fn(E)

    @block.scalar
    def _(E):
        for fn in b.ops["scalar"]:
            fn(E)

    @block.vector
    def _(E):
        for fn in b.ops["vector"]:
            fn(E)

    return nc


def kernel(**inputs):
    W = host_prep(inputs)
    nc = build()
    in_maps = []
    for bi in range(NB):
        for role in range(2):
            in_maps.append(per_core_inputs(inputs, W, bi, role))
    res = run_bass_kernel_spmd(nc, in_maps, core_ids=list(range(8)))
    out = np.zeros((NB, T, D), np.float32)
    for bi in range(NB):
        for role in range(2):
            o = res.results[bi * 2 + role]["out"]
            out[bi, role * TOWN:(role + 1) * TOWN, :] = np.asarray(o, np.float32).T
    return out
